# version 13
# baseline (speedup 1.0000x reference)
import numpy as np
import ml_dtypes
from contextlib import ExitStack

import concourse.bass as bass
import concourse.mybir as mybir
from concourse.bass_utils import run_bass_kernel_spmd

F32 = mybir.dt.float32
BF16 = mybir.dt.bfloat16
AX = mybir.AxisListType.X
OP = mybir.AluOpType
AF = mybir.ActivationFunctionType

P = 128
T = 4096
D = 1024
FF = 4096
NT = T // P
NCH = NT // 2
GAMMA = 0.96875
MAGIC = float(np.float32(3 * 2.0 ** 22))
LNEPS = 1e-5
QEPS = 1e-8
N_CORES = 8


def _gp(k):
    with np.errstate(under="ignore"):
        return float(np.float32(np.float64(GAMMA) ** k))


def _host_constants():
    t = np.arange(T, dtype=np.float64)[:, None]
    i = np.arange(D // 2, dtype=np.float64)[None, :]
    sv = (2.0 * i + 0.4 * D) / (1.4 * D)
    scale = sv ** (t / 512.0)
    inv_freq = 1.0 / 10000.0 ** (i / (D // 2))
    sinus = t * inv_freq
    sin_s = np.sin(sinus)
    cos_s = np.cos(sinus)

    def rep2(a):
        return np.repeat(a, 2, axis=1)

    gq = np.float64(GAMMA) ** (np.arange(T) % 256).astype(np.float64)
    gk = np.float64(GAMMA) ** (-(np.arange(T) % 128)).astype(np.float64)
    sgn = np.where(np.arange(D) % 2 == 0, -1.0, 1.0)[None, :]

    fcq = rep2(cos_s * scale) * gq[:, None]
    fsq = rep2(sin_s * scale) * gq[:, None] * sgn
    fck = rep2(cos_s / scale) * gk[:, None]
    fsk = rep2(sin_s / scale) * gk[:, None] * sgn

    tri = (np.arange(P)[None, :] >= np.arange(P)[:, None]).astype(np.float32)
    return {
        "fcq": fcq.astype(np.float32), "fsq": fsq.astype(np.float32),
        "fck": fck.astype(np.float32), "fsk": fsk.astype(np.float32),
        "idbf": np.eye(P, dtype=ml_dtypes.bfloat16),
        "idf32": np.eye(P, dtype=np.float32),
        "tri": tri,
    }


class _Ctr:

    def __init__(self, ctx, nc, name):
        self.h = ctx.enter_context(nc.semaphore(name))
        self.n = 0
        self.name = name


_SERIALIZE = False


class _Plan:

    ENGINES = ("sync", "scalar", "vector", "tensor", "gpsimd")

    def __init__(self, all_ctrs=None):
        self.steps = {e: [] for e in self.ENGINES}
        self.waited = {e: {} for e in self.ENGINES}
        self.all_ctrs = all_ctrs or []
        self.phase = "prolog"

    def em(self, eng, fn, waits=(), inc=None, inck=1):
        if _SERIALIZE and (_SERIALIZE is True or eng in _SERIALIZE or self.phase in _SERIALIZE):
            waits = list(waits) + [(c, c.n) for c in self.all_ctrs]
        ws = []
        for c, v in waits:
            if v <= 0:
                continue
            assert v <= c.n, f"wait on {c.name} for {v} but only {c.n} planned"
            prev = self.waited[eng].get(c.name, 0)
            if v > prev:
                ws.append((c, v))
                self.waited[eng][c.name] = v
        ih = None
        if inc is not None:
            inc.n += inck
            ih = (inc.h, inck)
        val = inc.n if inc is not None else None
        drain_inc = eng in ("vector", "scalar") and inck == 1

        def step(e, _ws=tuple(ws), _fn=fn, _ih=ih, _dr=drain_inc):
            for c, v in _ws:
                e.wait_ge(c.h, v)
            ins = _fn(e)
            if _ih is not None:
                if _dr:
                    e.drain().then_inc(_ih[0], _ih[1])
                else:
                    ins.then_inc(_ih[0], _ih[1])

        self.steps[eng].append(step)
        return val

    def vdrain(self):
        self.steps["vector"].append(lambda e: e.drain())

    def sdrain(self):
        self.steps["scalar"].append(lambda e: e.drain())

    def gdrain(self):
        self.steps["gpsimd"].append(lambda e: e.drain())

    def gfence(self, GE):
        GE.n += 1
        self.steps["gpsimd"].append(lambda e: e.drain().then_inc(GE.h, 1))
        return GE.n

    def tfence(self, TE):
        TE.n += 1
        v = TE.n
        self.steps["tensor"].append(lambda e, _v=(TE.h,): e.drain().then_inc(TE.h, 1))
        return v

    def wait_only(self, eng, waits):
        ws = []
        for c, v in waits:
            if v <= 0:
                continue
            prev = self.waited[eng].get(c.name, 0)
            if v > prev:
                ws.append((c, v))
                self.waited[eng][c.name] = v
        if not ws:
            return

        def step(e, _ws=tuple(ws)):
            for c, v in _ws:
                e.wait_ge(c.h, v)

        self.steps[eng].append(step)


_DEBUG = False


def _build():
    nc = bass.Bass("TRN2", target_bir_lowering=False)
    dbg_kind = "ExternalOutput" if _DEBUG else "Internal"

    ap = lambda t: t.ap()
    x_in = ap(nc.dram_tensor("x", [T, D], F32, kind="ExternalInput"))
    wq_in = ap(nc.dram_tensor("wq", [D, D], F32, kind="ExternalInput"))
    wk_in = ap(nc.dram_tensor("wk", [D, D], F32, kind="ExternalInput"))
    wv_in = ap(nc.dram_tensor("wv", [D, D], F32, kind="ExternalInput"))
    w1n_in = ap(nc.dram_tensor("w1n", [FF, D], F32, kind="ExternalInput"))
    w1t_in = ap(nc.dram_tensor("w1t", [D, FF], F32, kind="ExternalInput"))
    w2n_in = ap(nc.dram_tensor("w2n", [D, FF], F32, kind="ExternalInput"))
    w2t_in = ap(nc.dram_tensor("w2t", [FF, D], F32, kind="ExternalInput"))
    b1_in = ap(nc.dram_tensor("b1c", [P, FF // P], F32, kind="ExternalInput"))
    b2_in = ap(nc.dram_tensor("b2c", [P, D // P], F32, kind="ExternalInput"))
    fcq_in = ap(nc.dram_tensor("fcq", [T, D], F32, kind="ExternalInput"))
    fsq_in = ap(nc.dram_tensor("fsq", [T, D], F32, kind="ExternalInput"))
    fck_in = ap(nc.dram_tensor("fck", [T, D], F32, kind="ExternalInput"))
    fsk_in = ap(nc.dram_tensor("fsk", [T, D], F32, kind="ExternalInput"))
    idbf_in = ap(nc.dram_tensor("idbf", [P, P], BF16, kind="ExternalInput"))
    idf_in = ap(nc.dram_tensor("idf32", [P, P], F32, kind="ExternalInput"))
    tri_in = ap(nc.dram_tensor("tri", [P, P], F32, kind="ExternalInput"))
    out_hbm = ap(nc.dram_tensor("out", [T, D], F32, kind="ExternalOutput"))

    k_sp = ap(nc.dram_tensor("k_sp", [NT, P, D], BF16, kind=dbg_kind))
    x1_sp = ap(nc.dram_tensor("x1_sp", [NT, P, D], F32, kind=dbg_kind))
    v_sp = ap(nc.dram_tensor("v_sp", [NT, P, D], F32, kind=dbg_kind))
    h_sp = ap(nc.dram_tensor("h_sp", [FF // P, P, T], F32, kind=dbg_kind))
    w1i_sp = ap(nc.dram_tensor("w1i_sp", [D // P, P, FF], BF16, kind=dbg_kind))
    w2i_sp = ap(nc.dram_tensor("w2i_sp", [FF // P, P, D], BF16, kind=dbg_kind))
    vmx_hbm = ap(nc.dram_tensor("vmx_hbm", [P, 1], F32, kind=dbg_kind))
    hmx_hbm = ap(nc.dram_tensor("hmx_hbm", [P, 1], F32, kind=dbg_kind))
    cc1i = ap(nc.dram_tensor("cc1i", [1, 8], F32))
    cc1o = ap(nc.dram_tensor("cc1o", [1, 8], F32))
    cc2i = ap(nc.dram_tensor("cc2i", [1, 8], F32))
    cc2o = ap(nc.dram_tensor("cc2o", [1, 8], F32))
    if _DEBUG:
        dbg1 = ap(nc.dram_tensor("dbg1", [1, 8], F32, kind="ExternalOutput"))
        dbg2 = ap(nc.dram_tensor("dbg2", [1, 8], F32, kind="ExternalOutput"))
        dbgU = ap(nc.dram_tensor("dbgU", [P, D], BF16, kind="ExternalOutput"))
        dbgUFM = ap(nc.dram_tensor("dbgUFM", [P, D], BF16, kind="ExternalOutput"))
        dbgQP = ap(nc.dram_tensor("dbgQP", [P, D], BF16, kind="ExternalOutput"))
        dbgKP = ap(nc.dram_tensor("dbgKP", [P, D], BF16, kind="ExternalOutput"))
        dbgST = ap(nc.dram_tensor("dbgST", [P, 512], BF16, kind="ExternalOutput"))
        dbgA = ap(nc.dram_tensor("dbgA", [P, D], F32, kind="ExternalOutput"))
        dbgSW1 = ap(nc.dram_tensor("dbgSW1", [P, 32], F32, kind="ExternalOutput"))
        dbgRSW1 = ap(nc.dram_tensor("dbgRSW1", [P, 32], F32, kind="ExternalOutput"))
        dbgSW2 = ap(nc.dram_tensor("dbgSW2", [P, 8], F32, kind="ExternalOutput"))
        dbgRT = ap(nc.dram_tensor("dbgRT", [P, 512], F32, kind="ExternalOutput"))
        dbgS0 = ap(nc.dram_tensor("dbgS0", [P, 16], F32, kind="ExternalOutput"))
        dbgS1 = ap(nc.dram_tensor("dbgS1", [P, 16], F32, kind="ExternalOutput"))
        dbgS2 = ap(nc.dram_tensor("dbgS2", [P, 16], F32, kind="ExternalOutput"))
        dbgA1 = ap(nc.dram_tensor("dbgA1", [P, D], F32, kind="ExternalOutput"))
        dbgO2 = ap(nc.dram_tensor("dbgO2", [P, D], F32, kind="ExternalOutput"))
        dbgHQ = ap(nc.dram_tensor("dbgHQ", [P, 512], BF16, kind="ExternalOutput"))
        dbgS4 = ap(nc.dram_tensor("dbgS4", [P, 16], F32, kind="ExternalOutput"))
        dbgSR = ap(nc.dram_tensor("dbgSR", [P, D], F32, kind="ExternalOutput"))
        dbgBR = ap(nc.dram_tensor("dbgBR", [P, D], F32, kind="ExternalOutput"))
        dbgVQ = ap(nc.dram_tensor("dbgVQ", [P, 512], BF16, kind="ExternalOutput"))
        dbgFF = ap(nc.dram_tensor("dbgFF", [P, 68], F32, kind="ExternalOutput"))

    pl = _Plan()
    ev = {}

    with ExitStack() as gctx:
        sbg = lambda n, s, d: gctx.enter_context(nc.sbuf_tensor(n, s, d))
        psg = lambda n, s, d: gctx.enter_context(nc.psum_tensor(n, s, d))

        TE = _Ctr(gctx, nc, "TE")
        VE = _Ctr(gctx, nc, "VE")
        SC = _Ctr(gctx, nc, "SC")
        SY = _Ctr(gctx, nc, "SY")
        SD = _Ctr(gctx, nc, "SD")
        GD = _Ctr(gctx, nc, "GD")
        GE = _Ctr(gctx, nc, "GE")
        _dn = ["C0", "XI0", "XI1", "XI2", "XI3", "FL0", "FL1", "KSP", "XSP0", "XSP1", "VSP0", "VSP1",
               "KL0", "KL1", "KL2", "KL3", "WG0", "WG1", "WS", "RR", "VL0", "VL1", "HS0", "HS1",
               "HL0", "HL1", "HL2", "HL3", "XR0", "XR1", "XR2", "XR3", "OT0", "OT1", "W1L", "W2L"]
        dmac = {n: _Ctr(gctx, nc, n) for n in _dn}
        pl.all_ctrs = [TE, VE, SC, SY, SD, GD, GE] + list(dmac.values())

        PF = [psg(f"pf{i}", [P, 512], F32) for i in range(6)]
        PB = [psg(f"pb{i}", [P, 1024], BF16) for i in range(2)]

        idbf = sbg("idbf_t", [P, P], BF16)
        idf = sbg("idf_t", [P, P], F32)
        trit = sbg("tri_t", [P, P], F32)
        onesrow = sbg("onesrow", [1, P], F32)
        onecol = sbg("onecol", [P, 1], F32)
        epscol = sbg("epscol", [P, 1], F32)
        nmagcol = sbg("nmagcol", [P, 1], F32)
        zerocol = sbg("zerocol", [P, 1], F32)
        nhalfcol = sbg("nhalfcol", [P, 1], F32)
        nonecol = sbg("nonecol", [P, 1], F32)
        b1c = sbg("b1c_t", [P, 32], F32)
        b2c = sbg("b2c_t", [P, 8], F32)
        sw1c = sbg("sw1c", [P, 32], F32)
        sw2c = sbg("sw2c", [P, 8], F32)
        rsw1c = sbg("rsw1c", [P, 32], F32)
        rsw2c = sbg("rsw2c", [P, 8], F32)
        ss1c = sbg("ss1c", [P, 32], F32)
        ss2c = sbg("ss2c", [P, 8], F32)
        bq1c = sbg("bq1c", [P, 32], F32)
        bq2c = sbg("bq2c", [P, 8], F32)
        s1col = sbg("s1col", [P, 1], F32)
        rs1col = sbg("rs1col", [P, 1], F32)
        s2col = sbg("s2col", [P, 1], F32)
        rs2col = sbg("rs2col", [P, 1], F32)
        vmaxc = sbg("vmaxc", [P, 1], F32)
        hmaxc = sbg("hmaxc", [P, 1], F32)
        rmaxc = sbg("rmaxc", [P, 1], F32)
        prmax = sbg("prmax", [P, 1], F32)
        meanb = sbg("meanb", [P, 4], F32)
        ex2b = sbg("ex2b", [P, 4], F32)
        msqb = sbg("msqb", [P, 4], F32)
        varb = sbg("varb", [P, 4], F32)
        sdb = sbg("sdb", [P, 4], F32)
        bn6 = sbg("bn6", [P, 48], F32)
        mvb = sbg("mvb", [P, 8], F32)
        vrow = sbg("vrow", [1, P], F32)
        scrow = sbg("scrow", [1, 8], F32)
        grow = sbg("grow", [1, 8], F32)
        row2 = sbg("row2", [1, 2], F32)
        rowst = sbg("rowst", [8, P], F32)

        C0 = dmac["C0"]
        pl.em("sync", lambda e: e.dma_start(out=idbf[:], in_=idbf_in), inc=C0, inck=16)
        pl.em("sync", lambda e: e.dma_start(out=idf[:], in_=idf_in), inc=C0, inck=16)
        pl.em("sync", lambda e: e.dma_start(out=trit[:], in_=tri_in), inc=C0, inck=16)
        pl.em("sync", lambda e: e.dma_start(out=b1c[:], in_=b1_in), inc=C0, inck=16)
        pl.em("sync", lambda e: e.dma_start(out=b2c[:], in_=b2_in), inc=C0, inck=16)
        for _k in ("idbf", "idf", "tri", "b1", "b2"):
            ev[_k] = (C0, C0.n)

        def _init(e):
            e.memset(onesrow[:], 1.0)
            e.memset(onecol[:], 1.0)
            e.memset(epscol[:], LNEPS)
            e.memset(nmagcol[:], -MAGIC)
            e.memset(zerocol[:], 0.0)
            e.memset(nhalfcol[:], -0.5)
            e.memset(nonecol[:], -1.0)
            e.memset(vmaxc[:], -3.0e38)
            return e.memset(hmaxc[:], -3.0e38)
        ev["init"] = (VE, pl.em("vector", _init, inc=VE))

        with ExitStack() as actx:
            sba = lambda n, s, d: actx.enter_context(nc.sbuf_tensor(n, s, d))
            wqb = sba("wqb", [P, 8 * D], BF16)
            wkb = sba("wkb", [P, 8 * D], BF16)
            wvb = sba("wvb", [P, 8 * D], BF16)
            vres = sba("vres", [P, NT * D], BF16)
            stw = sba("stw", [P, NT * 256], BF16)
            qfm = sba("qfm", [P, 8 * 256], BF16)
            kstr = [sba(f"kstr{i}", [P, D], BF16) for i in range(4)]
            fld = [sba(f"fld{i}", [P, 2 * D], F32) for i in range(2)]
            xin = [sba(f"xin{i}", [P, D], F32) for i in range(4)]
            ubf = sba("ubf", [P, D], BF16)
            ufm = sba("ufm", [P, D], BF16)
            qpb = sba("qpb", [P, D], BF16)
            kpb = sba("kpb", [P, D], BF16)
            kfmt = sba("kfmt", [P, D], BF16)
            rotb = sba("rotb", [P, 512], BF16)
            asb = [sba(f"asb{i}", [P, D], F32) for i in range(2)]
            x1t = asb
            s1c_ = sba("s1c_", [P, 4], F32)
            q1c_ = sba("q1c_", [P, 4], F32)
            s2c_ = sba("s2c_", [P, 2], F32)
            q2c_ = sba("q2c_", [P, 2], F32)
            s3c_ = sba("s3c_", [P, 2], F32)
            q3c_ = sba("q3c_", [P, 2], F32)
            rstd1 = sba("rstd1", [P, 4], F32)
            nmr1 = sba("nmr1", [P, 4], F32)
            rstd2 = sba("rstd2", [P, 2], F32)
            nmr2 = sba("nmr2", [P, 2], F32)
            rstd3 = sba("rstd3", [P, 2], F32)
            nmr3 = sba("nmr3", [P, 2], F32)
            wstg = [sba(f"wstg{i}", [P, 512], F32) for i in range(2)]
            wstg_free_ctr = [None, None]
            rswt = [sba(f"rswt{i}", [P, 512], F32) for i in range(2)]
            wibf = sba("wibf", [P, 512], BF16)
            rswrow = sba("rswrow", [1, 512], F32)

            for name, win, wsb in (("wq", wq_in, wqb), ("wk", wk_in, wkb), ("wv", wv_in, wvb)):
                _gv = pl.em(
                    "gpsimd",
                    lambda e, win=win, wsb=wsb: e.dma_start(
                        out=wsb[:].rearrange("p (c j) -> p c j", c=8),
                        in_=win.rearrange("(c p) j -> p c j", p=128)),
                    inc=GD, inck=16)
            for name in ("wq", "wk", "wv"):
                ev[f"{name}b"] = (GD, _gv)

            def ln_chain(eng_s, s_ap, q_ap, k, rstd_ap, nmr_ap, dsize, waits):
                pl.vdrain()
                pl.em("vector", lambda e: e.tensor_scalar(meanb[:, :k], s_ap, 1.0 / dsize, None, OP.mult),
                      waits=list(waits) + [ev.get("chain_busy", (SC, 0))])
                pl.em("vector", lambda e: e.tensor_scalar(ex2b[:, :k], q_ap, 1.0 / dsize, None, OP.mult))
                pl.vdrain()
                pl.em("vector", lambda e: e.tensor_mul(msqb[:, :k], meanb[:, :k], meanb[:, :k]))
                pl.vdrain()
                v1 = pl.em("vector", lambda e: e.tensor_tensor(varb[:, :k], ex2b[:, :k], msqb[:, :k], OP.subtract), inc=VE)
                pl.em("scalar", lambda e: e.activation(sdb[:, :k], varb[:, :k], AF.Ln, bias=epscol[:], scale=onecol[:]),
                      waits=[(VE, v1)])
                pl.sdrain()
                ev["chain_busy"] = (SC, SC.n)
                s1 = pl.em("scalar", lambda e: e.activation(rstd_ap, sdb[:, :k], AF.Exp, bias=zerocol[:], scale=nhalfcol[:]),
                           inc=SC)
                v2 = pl.em("vector", lambda e: e.scalar_tensor_tensor(nmr_ap, meanb[:, :k], -1.0, rstd_ap, OP.mult, OP.mult),
                           waits=[(SC, s1)], inc=VE)
                return v2

            def bn_chain(tile_aps, k, rstd_ap, nmr_ap, waits_list):
                for j in range(k):
                    w = list(waits_list[j]) + ([ev.get("chain_busy", (SC, 0))] if j == 0 else [])
                    pl.em("vector", lambda e, j=j: e.bn_stats(bn6[:, j * 12: j * 12 + 6],
                                                              tile_aps[j][:, 0:512]), waits=w)
                    pl.em("vector", lambda e, j=j: e.bn_stats(bn6[:, j * 12 + 6: j * 12 + 12],
                                                              tile_aps[j][:, 512:1024]))
                pl.vdrain()
                for j in range(k):
                    v1 = pl.em("vector", lambda e, j=j: e.bn_aggr(mvb[:, j * 2: j * 2 + 2],
                                                                  bn6[:, j * 12: j * 12 + 12]),
                               inc=VE if j == k - 1 else None)
                pl.em("scalar", lambda e: e.activation(sdb[:, :k], mvb[:, 1:2 * k:2], AF.Ln,
                                                       bias=epscol[:], scale=onecol[:]),
                      waits=[(VE, v1)])
                pl.sdrain()
                s1 = pl.em("scalar", lambda e: e.activation(rstd_ap, sdb[:, :k], AF.Exp,
                                                            bias=zerocol[:], scale=nhalfcol[:]), inc=SC)
                ev["chain_busy"] = (SC, s1)
                v2 = pl.em("vector", lambda e: e.scalar_tensor_tensor(nmr_ap, mvb[:, 0:2 * k:2], -1.0,
                                                                      rstd_ap, OP.mult, OP.mult),
                           waits=[(SC, s1)], inc=VE)
                return v2

            tr_free = [ev["init"][1], ev["init"][1]]
            tr_buf = [0]

            def trans8(src_bf, dst3d_fn, src_ready, extra_waits=()):
                b = tr_buf[0] % 2
                tr_buf[0] += 1
                waits = [src_ready, (VE, tr_free[b]), ev["idbf"]] + list(extra_waits)
                for dc in range(8):
                    t_inc = TE if dc == 7 else None
                    tv = pl.em("tensor",
                               lambda e, dc=dc, b=b, s=src_bf: e.transpose(
                                   PB[b][:, dc * 128:(dc + 1) * 128], s[:, dc * 128:(dc + 1) * 128], idbf[:]),
                               waits=waits if dc == 0 else (), inc=t_inc)
                fv = pl.tfence(TE)
                vv = pl.em("vector", lambda e, b=b: dst3d_fn(e, PB[b]), waits=[(TE, fv)], inc=VE)
                tr_free[b] = vv
                return vv

            wsteps = []
            wstg_free = [ev["init"][1], ev["init"][1]]

            def _mk_sw1(fc, hh):
                def f():
                    b = (2 * fc + hh) % 2
                    WGb = dmac[f"WG{b}"]
                    gv = pl.em("sync",
                               lambda e: e.dma_start(out=wstg[b][:],
                                                     in_=w1n_in[fc * P:(fc + 1) * P, hh * 512:(hh + 1) * 512]),
                               waits=[(wstg_free_ctr[b] or VE, wstg_free[b])], inc=WGb, inck=16)
                    if hh == 0:
                        vv = pl.em("vector", lambda e: e.reduce_max(sw1c[:, fc:fc + 1], wstg[b][:], axis=AX,
                                                                    apply_absolute_value=True),
                                   waits=[(WGb, gv)], inc=VE)
                    else:
                        pl.em("vector", lambda e: e.reduce_max(rmaxc[:], wstg[b][:], axis=AX,
                                                               apply_absolute_value=True), waits=[(WGb, gv)])
                        pl.vdrain()
                        vv = pl.em("vector", lambda e: e.tensor_max(sw1c[:, fc:fc + 1], sw1c[:, fc:fc + 1], rmaxc[:]),
                                   inc=VE)
                    wstg_free[b] = vv
                    wstg_free_ctr[b] = VE
                    ev["sw_scan"] = (VE, vv)
                return f

            for fc in range(32):
                for hh in range(2):
                    wsteps.append(_mk_sw1(fc, hh))

            def _mk_sw2(ec, q):
                def f():
                    b = (8 * ec + q) % 2
                    WGb = dmac[f"WG{b}"]
                    gv = pl.em("sync",
                               lambda e: e.dma_start(out=wstg[b][:],
                                                     in_=w2n_in[ec * P:(ec + 1) * P, q * 512:(q + 1) * 512]),
                               waits=[(wstg_free_ctr[b] or VE, wstg_free[b])], inc=WGb, inck=16)
                    if q == 0:
                        vv = pl.em("vector", lambda e: e.reduce_max(sw2c[:, ec:ec + 1], wstg[b][:], axis=AX,
                                                                    apply_absolute_value=True),
                                   waits=[(WGb, gv)], inc=VE)
                    else:
                        pl.em("vector", lambda e: e.reduce_max(rmaxc[:], wstg[b][:], axis=AX,
                                                               apply_absolute_value=True), waits=[(WGb, gv)])
                        pl.vdrain()
                        vv = pl.em("vector", lambda e: e.tensor_max(sw2c[:, ec:ec + 1], sw2c[:, ec:ec + 1], rmaxc[:]),
                                   inc=VE)
                    wstg_free[b] = vv
                    wstg_free_ctr[b] = VE
                    ev["sw_scan"] = (VE, vv)
                return f

            for ec in range(8):
                for q in range(8):
                    wsteps.append(_mk_sw2(ec, q))

            def _scales():
                pl.vdrain()
                pl.em("vector", lambda e: e.tensor_scalar(sw1c[:], sw1c[:], QEPS, 1.0 / 127.0, OP.max, OP.mult),
                      waits=[ev["sw_scan"]])
                pl.vdrain()
                vv = pl.em("vector", lambda e: e.tensor_scalar(sw2c[:], sw2c[:], QEPS, 1.0 / 127.0, OP.max, OP.mult),
                           inc=VE)
                pl.em("scalar", lambda e: e.activation(rsw1c[:], sw1c[:], AF.Ln, bias=zerocol[:], scale=onecol[:]),
                      waits=[(VE, vv)])
                pl.em("scalar", lambda e: e.activation(rsw2c[:], sw2c[:], AF.Ln, bias=zerocol[:], scale=onecol[:]))
                pl.sdrain()
                pl.em("scalar", lambda e: e.activation(rsw1c[:], rsw1c[:], AF.Exp, bias=zerocol[:], scale=nonecol[:]))
                v = pl.em("scalar", lambda e: e.activation(rsw2c[:], rsw2c[:], AF.Exp, bias=zerocol[:], scale=nonecol[:]),
                          inc=SC)
                ev["scales"] = (SC, v)
                if _DEBUG:
                    pl.em("sync", lambda e: e.dma_start(out=dbgSW1, in_=sw1c[:]), waits=[(VE, v)], inc=SY, inck=16)
                    pl.em("sync", lambda e: e.dma_start(out=dbgRSW1, in_=rsw1c[:]), inc=SY, inck=16)
                    pl.em("sync", lambda e: e.dma_start(out=dbgSW2, in_=sw2c[:]), inc=SY, inck=16)
            wsteps.append(_scales)

            rt_state = {"cur": 0, "free": [(VE, 0), (VE, 0)]}

            def _mk_rowtile(col_ap_fn):
                def f():
                    b = rt_state["cur"] ^ 1
                    rt_state["cur"] = b
                    tv = pl.em("tensor", lambda e: e.transpose(PF[4][:4, :P], col_ap_fn(), idf[:]),
                               waits=[ev["scales"], ev["idf"],
                                      ev.get("rswrow_free", (TE, 0)), rt_state["free"][b],
                                      (psq_free_ctr[0], psq_free[0]), (psq_free_ctr[1], psq_free[1])], inc=TE)
                    tv = pl.tfence(TE)
                    vv = pl.em("vector", lambda e: e.tensor_copy(rowst[:4, :], PF[4][:4, :P]),
                               waits=[(TE, tv), ev.get("rowst_free", (VE, 0))], inc=VE)
                    for cc in range(4):
                        g1 = pl.em("sync",
                                   lambda e, cc=cc: e.dma_start(out=rswrow[:, cc * P:(cc + 1) * P],
                                                                in_=rowst[cc:cc + 1, :]),
                                   waits=[(VE, vv)] if cc == 0 else [], inc=dmac["RR"], inck=16)
                    ev["rowst_free"] = (dmac["RR"], g1)
                    t1 = pl.em("tensor",
                               lambda e: e.matmul(PF[5][:, :512], onesrow[:], rswrow[:], start=True, stop=True),
                               waits=[(dmac["RR"], g1), (VE, ev["init"][1])], inc=TE)
                    t1 = pl.tfence(TE)
                    v1 = pl.em("vector", lambda e, b=b: e.tensor_copy(rswt[b][:], PF[5][:]), waits=[(TE, t1)], inc=VE)
                    psq_free[0] = v1
                    psq_free[1] = v1
                    psq_free_ctr[0] = VE
                    psq_free_ctr[1] = VE
                    ev["rswrow_free"] = (TE, t1)
                    ev["rowtile"] = (VE, v1)
                    if _DEBUG and ev.get("rt_dumped") is None:
                        ev["rt_dumped"] = True
                        pl.em("sync", lambda e, b=b: e.dma_start(out=dbgRT, in_=rswt[b][:]), waits=[(VE, v1)], inc=SY, inck=16)
                return f

            def _mk_wq(src_in, sp_out_fn, evkey):
                def f():
                    b = next(_wqb)
                    WGb = dmac[f"WG{b}"]
                    gv = pl.em("sync", lambda e: e.dma_start(out=wstg[b][:], in_=src_in),
                               waits=[(wstg_free_ctr[b] or VE, wstg_free[b])], inc=WGb, inck=16)
                    b_rt = rt_state["cur"]
                    pl.em("gpsimd", lambda e, b=b, b_rt=b_rt: e.tensor_mul(wstg[b][:], wstg[b][:], rswt[b_rt][:]),
                          waits=[(WGb, gv), ev["rowtile"]])
                    pl.gdrain()
                    pl.em("gpsimd", lambda e, b=b: e.tensor_scalar(wstg[b][:], wstg[b][:], 1.0, MAGIC,
                                                                   OP.mult, OP.add))
                    pl.gdrain()
                    pl.em("gpsimd", lambda e, b=b: e.tensor_scalar(wibf[:], wstg[b][:], -MAGIC, None, OP.add),
                          waits=[ev.get("wibf_free", (VE, 0))])
                    vv = pl.gfence(GE)
                    rt_state["free"][b_rt] = (GE, vv)
                    wstg_free[b] = vv
                    wstg_free_ctr[b] = GE
                    gv2 = pl.em("scalar", lambda e: e.dma_start(out=sp_out_fn(), in_=wibf[:]),
                                waits=[(GE, vv)], inc=dmac["WS"], inck=16)
                    ev[evkey] = (dmac["WS"], gv2)
                    ev["wibf_free"] = (dmac["WS"], gv2)
                return f

            _wqb_ctr = [0]
            def _next_wqb():
                while True:
                    _wqb_ctr[0] ^= 1
                    yield _wqb_ctr[0]
            _wqb = _next_wqb()

            for r in range(8):
                wsteps.append(_mk_rowtile(lambda r=r: rsw1c[:, r * 4:(r + 1) * 4]))
                for dc in range(8):
                    wsteps.append(_mk_wq(w1t_in[dc * P:(dc + 1) * P, r * 512:(r + 1) * 512],
                                         lambda dc=dc, r=r: w1i_sp[dc, :, r * 512:(r + 1) * 512],
                                         ("w1isp", r, dc)))
            for h in range(2):
                wsteps.append(_mk_rowtile(lambda h=h: rsw2c[:, h * 4:(h + 1) * 4]))
                for fc in range(32):
                    wsteps.append(_mk_wq(w2t_in[fc * P:(fc + 1) * P, h * 512:(h + 1) * 512],
                                         lambda fc=fc, h=h: w2i_sp[fc, :, h * 512:(h + 1) * 512],
                                         ("w2isp", h, fc)))

            wstep_iter = iter(wsteps)
            n_wsteps = len(wsteps)
            per_chunk = (n_wsteps + NCH - 1) // NCH

            psq_free = [ev["init"][1], ev["init"][1]]
            psq_free_ctr = [VE, VE]
            pss_free = [(VE, 0), (VE, 0), (VE, 0)]
            def emit_ln1_stats(c):
                cp = c % 2
                ws = []
                for j, t in enumerate((2 * c, 2 * c + 1)):
                    lw = [ev.get(("xin_free", t - 4), (VE, 0))]
                    XIb = dmac[f"XI{t % 4}"]
                    dv = pl.em("sync", lambda e, t=t, b=t % 4: e.dma_start(out=xin[b][:], in_=x_in[t * P:(t + 1) * P, :]),
                               waits=lw, inc=XIb, inck=16)
                    ev[("xin", t)] = (XIb, dv)
                    ws.append([(XIb, dv)])
                ev[("ln1cv", c)] = (VE, bn_chain(
                    [xin[(2 * c) % 4], xin[(2 * c + 1) % 4]], 2,
                    rstd1[:, cp * 2:cp * 2 + 2], nmr1[:, cp * 2:cp * 2 + 2], ws))

            def emit_tile_prod(c):
                pl.phase = "attn"
                cp = c % 2
                tiles = (2 * c, 2 * c + 1)
                cv = ev[("ln1cv", c)][1]
                if _DEBUG and c in (0, 1):
                    dst = dbgS0 if c == 0 else dbgS1
                    pl.em("sync", lambda e, dst=dst: e.dma_start(out=dst[:, 0:2], in_=s1c_[:]), waits=[(VE, cv)], inc=SY, inck=16)
                    pl.em("sync", lambda e, dst=dst: e.dma_start(out=dst[:, 2:4], in_=q1c_[:]), inc=SY, inck=16)
                    pl.em("sync", lambda e, dst=dst: e.dma_start(out=dst[:, 4:6], in_=rstd1[:]), inc=SY, inck=16)
                    pl.em("sync", lambda e, dst=dst: e.dma_start(out=dst[:, 6:8], in_=nmr1[:]), inc=SY, inck=16)
                for j, t in enumerate(tiles):
                    b = t % 4
                    uv = pl.em("scalar", lambda e, b=b, jj=cp * 2 + j: e.activation(ubf[:], xin[b][:], AF.Identity,
                                                                          bias=nmr1[:, jj:jj + 1], scale=rstd1[:, jj:jj + 1]),
                               waits=[(VE, cv), ev.get(("ubf_free", t - 1), (TE, 0))], inc=SC)
                    uf = trans8(ubf, lambda e, pb: e.tensor_copy(ufm[:], pb[:]), (SC, uv),
                                extra_waits=[ev.get(("ufm_free", t - 1), (TE, 0))])
                    ev[("ubf_free", t)] = (TE, TE.n)
                    if _DEBUG and t == 0:
                        pl.em("sync", lambda e: e.dma_start(out=dbgU, in_=ubf[:]), waits=[(SC, uv)], inc=SY, inck=16)
                        pl.em("sync", lambda e: e.dma_start(out=dbgUFM, in_=ufm[:]), waits=[(VE, uf)], inc=SY, inck=16)
                    for pi, (cf, sf) in enumerate(((fcq_in, fsq_in), (fck_in, fsk_in))):
                        fb = (2 * t + pi) % 2
                        fw = [ev.get(("fld_free", 2 * t + pi - 2), (VE, 0))]
                        FLb = dmac[f"FL{fb}"]
                        pl.em("sync", lambda e, fb=fb, cf=cf, t=t: e.dma_start(out=fld[fb][:, :D], in_=cf[t * P:(t + 1) * P, :]),
                              waits=fw, inc=FLb, inck=16)
                        dv = pl.em("sync", lambda e, fb=fb, sf=sf, t=t: e.dma_start(out=fld[fb][:, D:], in_=sf[t * P:(t + 1) * P, :]),
                                   inc=FLb, inck=16)
                        ev[("fld", 2 * t + pi)] = (FLb, dv)
                    for wi, (wsb, kind) in enumerate(((wqb, "q"), (wkb, "k"), (wvb, "v"))):
                        for h in range(2):
                            g = (wi * 2 + h)
                            pb_ = PF[4 + (g % 2)]
                            gfree = psq_free[g % 2]
                            gctr = psq_free_ctr[g % 2]
                            for dc in range(8):
                                waits = []
                                if dc == 0:
                                    waits = [(VE, uf), (gctr, gfree), (GD, ev[f"w{kind}b"][1])]
                                tv = pl.em("tensor",
                                           lambda e, pb_=pb_, dc=dc, wsb=wsb, h=h: e.matmul(
                                               pb_[:], ufm[:, dc * 128:(dc + 1) * 128],
                                               wsb[:, dc * D + h * 512: dc * D + (h + 1) * 512],
                                               start=(dc == 0), stop=(dc == 7), skip_group_check=True),
                                           waits=waits, inc=TE if dc == 7 else None)
                            fv = pl.tfence(TE)
                            if kind == "v":
                                sv = pl.em("scalar",
                                           lambda e, pb_=pb_, t=t, h=h: e.copy(
                                               vres[:, t * D + h * 512: t * D + (h + 1) * 512], pb_[:]),
                                           waits=[(TE, fv)], inc=SC)
                                psq_free[g % 2] = sv
                                psq_free_ctr[g % 2] = SC
                                if h == 1:
                                    ev[("vres", t)] = (SC, sv)
                            else:
                                dst = qpb if kind == "q" else kpb
                                fb = (2 * t + (0 if kind == "q" else 1)) % 2
                                fe = ev[("fld", 2 * t + (0 if kind == "q" else 1))]
                                pl.em("vector",
                                      lambda e, pb_=pb_, fb=fb, h=h, dst=dst: e.tensor_mul(
                                          dst[:, h * 512:(h + 1) * 512], pb_[:], fld[fb][:, h * 512:(h + 1) * 512]),
                                      waits=[(TE, fv), fe,
                                             ev.get(("dst_free", kind, t - 1), (TE, 0))])
                                pl.em("vector",
                                      lambda e, pb_=pb_, fb=fb, h=h: e.tensor_mul(
                                          rotb[:, 0:512:2], pb_[:, 1:512:2], fld[fb][:, D + h * 512: D + (h + 1) * 512][:, 0:512:2]))
                                pl.em("vector",
                                      lambda e, pb_=pb_, fb=fb, h=h: e.tensor_mul(
                                          rotb[:, 1:512:2], pb_[:, 0:512:2], fld[fb][:, D + h * 512: D + (h + 1) * 512][:, 1:512:2]))
                                pl.vdrain()
                                vv = pl.em("vector",
                                           lambda e, dst=dst, h=h: e.tensor_add(
                                               dst[:, h * 512:(h + 1) * 512], dst[:, h * 512:(h + 1) * 512], rotb[:]), inc=VE)
                                psq_free[g % 2] = vv
                                psq_free_ctr[g % 2] = VE
                                if h == 1:
                                    ev[("xpos", kind, t)] = (VE, vv)
                                    ev[("fld_free", 2 * t + (0 if kind == "q" else 1))] = (VE, vv)
                                    if _DEBUG and t == 0:
                                        pl.em("sync", lambda e, dst=dst, kind=kind: e.dma_start(
                                            out=dbgQP if kind == "q" else dbgKP, in_=dst[:]),
                                            waits=[(VE, vv)], inc=SY, inck=16)
                    qv = trans8(qpb,
                                lambda e, pb, j=j: e.tensor_copy(
                                    qfm[:].rearrange("p (c w t) -> p c w t", c=8, w=2)[:, :, j],
                                    pb[:].rearrange("p (c t) -> p c t", c=8)),
                                ev[("xpos", "q", t)],
                                extra_waits=[ev.get(("qfm_free", c - 1), (TE, 0))])
                    ev[("dst_free", "q", t)] = (TE, TE.n)
                    ev[("qfm", t)] = (VE, qv)
                    kv_ = trans8(kpb,
                                 lambda e, pb: e.tensor_copy(kfmt[:], pb[:]),
                                 ev[("xpos", "k", t)],
                                 extra_waits=[ev.get(("kfmt_free", t - 1), (SY, 0))])
                    ev[("dst_free", "k", t)] = (TE, TE.n)
                    dv = pl.em("sync", lambda e, t=t: e.dma_start(out=k_sp[t], in_=kfmt[:]),
                               waits=[(VE, kv_)], inc=dmac["KSP"], inck=16)
                    ev[("ksp", t)] = (dmac["KSP"], dv)
                    ev[("kfmt_free", t)] = (dmac["KSP"], dv)

            emit_ln1_stats(0)
            emit_tile_prod(0)
            for c in range(NCH):
                cp = c % 2
                tiles = (2 * c, 2 * c + 1)
                if c + 1 < NCH:
                    emit_ln1_stats(c + 1)
                pl.phase = "scores"
                nmax = 2 * c + 1
                pss_free[2] = (psq_free_ctr[0], psq_free[0])

                def emit_kload(cc, m):
                    if ("kload", cc, m) in ev:
                        return
                    kn = cc * (cc + 1) + m
                    mb = m % 4
                    KLb = dmac[f"KL{mb}"]
                    dv = pl.em("scalar", lambda e, mb=mb, m=m: e.dma_start(out=kstr[mb][:], in_=k_sp[m]),
                               waits=[ev[("ksp", m)], ev.get(("kstr_free", kn - 4), (TE, 0))],
                               inc=KLb, inck=16)
                    ev[("kload", cc, m)] = (KLb, dv)

                for m0 in range(min(4, nmax + 1)):
                    emit_kload(c, m0)

                def emit_scmm(m):
                    ms = m % 3
                    kn = c * (c + 1) + m
                    first_inc = None
                    for dc in range(8):
                        waits = []
                        if dc == 0:
                            waits = [ev[("kload", c, m)], pss_free[ms], (VE, ev[("qfm", tiles[1])][1])]
                        tv = pl.em("tensor",
                                   lambda e, ms=ms, mb=m % 4, dc=dc: e.matmul(
                                       PF[2 + ms][:, 0:256], kstr[mb][:, dc * 128:(dc + 1) * 128],
                                       qfm[:, dc * 256:(dc + 1) * 256],
                                       start=(dc == 0), stop=(dc == 7), skip_group_check=True),
                                   waits=waits, inc=TE)
                        if dc == 0:
                            first_inc = tv
                    ev[("scmm", c, m)] = (TE, tv)
                    ev[("kstr_free", kn)] = (TE, tv)
                    if m + 4 <= nmax:
                        emit_kload(c, m + 4)
                    return first_inc

                def emit_stev(m, lag_ev):
                    ms = m % 3
                    g = _gp(128 * (2 * c - m))
                    if m == 2 * c:
                        pl.em("vector",
                              lambda e, ms=ms, m=m: e.scalar_tensor_tensor(
                                  stw[:, m * 256: m * 256 + 128], PF[2 + ms][:, 0:128], 1.0, trit[:],
                                  OP.mult, OP.mult),
                              waits=[lag_ev, ev["tri"], ev.get(("stw_free", c - 1), (TE, 0))])
                        vv = pl.em("vector",
                                   lambda e, ms=ms, m=m: e.tensor_copy(
                                       stw[:, m * 256 + 128: (m + 1) * 256], PF[2 + ms][:, 128:256]), inc=VE)
                    elif m == 2 * c + 1:
                        vv = pl.em("vector",
                                   lambda e, ms=ms, m=m, g=_gp(-128): e.scalar_tensor_tensor(
                                       stw[:, m * 256 + 128: (m + 1) * 256], PF[2 + ms][:, 128:256], g, trit[:],
                                       OP.mult, OP.mult),
                                   waits=[lag_ev, ev["tri"], ev.get(("stw_free", c - 1), (TE, 0))], inc=VE)
                    else:
                        vv = pl.em("vector",
                                   lambda e, ms=ms, m=m, g=g: e.tensor_scalar(
                                       stw[:, m * 256: (m + 1) * 256], PF[2 + ms][:, 0:256], g, None, OP.mult),
                                   waits=[lag_ev, ev.get(("stw_free", c - 1), (TE, 0))], inc=VE)
                    ev[("stev", c, m)] = (VE, vv)
                    pss_free[ms] = (VE, vv)
                    if ms == 2:
                        psq_free[0] = vv
                        psq_free_ctr[0] = VE
                    if _DEBUG and c == 0:
                        pl.em("sync", lambda e, m=m: e.dma_start(out=dbgST[:, m * 256:(m + 1) * 256],
                                                                 in_=stw[:, m * 256:(m + 1) * 256]),
                              waits=[(VE, vv)], inc=SY, inck=16)

                def emit_av1(m):
                    av_first = None
                    for nsub in range(2):
                        nt_ = 2 * c + nsub
                        if nt_ < m:
                            continue
                        waits = [ev[("stev", c, m)], ev[("vres", m)]]
                        if m == 0:
                            waits.append(ev.get(("af", c - 1), (SC, 0)))
                        tv2 = pl.em("tensor",
                                    lambda e, nsub=nsub, m=m, nt_=nt_: e.matmul(
                                        PF[nsub][:], stw[:, m * 256 + nsub * 128: m * 256 + (nsub + 1) * 128],
                                        vres[:, m * D: m * D + 512],
                                        start=(m == 0), stop=(m == nt_), skip_group_check=True),
                                    waits=waits, inc=TE)
                        if av_first is None:
                            av_first = tv2
                    return av_first

                for m in range(nmax + 1):
                    fi = emit_scmm(m)
                    if m >= 1:
                        emit_stev(m - 1, (TE, fi))
                    if m >= 2:
                        emit_av1(m - 2)
                fi = emit_av1(nmax - 1)
                emit_stev(nmax, (TE, fi))
                emit_av1(nmax)
                ev[("avp1", c)] = (TE, pl.tfence(TE))
                a1 = pl.em("scalar", lambda e: e.copy(asb[0][:, 0:512], PF[0][:]),
                           waits=[(TE, ev[("avp1", c)][1]), ev.get(("asb_free", c - 1, 0), (VE, 0)),
                                  ev.get(("asb_free_sp", c - 1, 0), (VE, 0))], inc=SC)
                a2 = pl.em("scalar", lambda e: e.copy(asb[1][:, 0:512], PF[1][:]),
                           waits=[ev.get(("asb_free", c - 1, 1), (VE, 0)),
                                  ev.get(("asb_free_sp", c - 1, 1), (VE, 0))], inc=SC)
                ev[("aev1", c)] = (SC, a2)
                for m in range(nmax + 1):
                    for nsub in range(2):
                        nt_ = 2 * c + nsub
                        if nt_ < m:
                            continue
                        waits = []
                        if m == 0:
                            waits = [ev[("aev1", c)]]
                        tv2 = pl.em("tensor",
                                    lambda e, nsub=nsub, m=m, nt_=nt_: e.matmul(
                                        PF[nsub][:], stw[:, m * 256 + nsub * 128: m * 256 + (nsub + 1) * 128],
                                        vres[:, m * D + 512: (m + 1) * D],
                                        start=(m == 0), stop=(m == nt_), skip_group_check=True),
                                    waits=waits, inc=TE)
                ev[("avp2", c)] = (TE, pl.tfence(TE))
                ev[("stw_free", c)] = (TE, TE.n)
                a1 = pl.em("scalar", lambda e: e.copy(asb[0][:, 512:], PF[0][:]),
                           waits=[(TE, ev[("avp2", c)][1])], inc=SC)
                a2 = pl.em("scalar", lambda e: e.copy(asb[1][:, 512:], PF[1][:]), inc=SC)
                ev[("af", c)] = (SC, a2)
                if _DEBUG and c == 0:
                    pl.em("sync", lambda e: e.dma_start(out=dbgA, in_=asb[0][:]), waits=[(SC, a1)], inc=SY, inck=16)

                if c + 1 < NCH:
                    emit_tile_prod(c + 1)
                    for m0 in range(4):
                        emit_kload(c + 1, m0)
                pl.phase = "epi"
                cv2 = bn_chain([asb[0], asb[1]], 2, rstd2[:], nmr2[:],
                               [[(SC, a1)], [ev[("af", c)]]])
                if _DEBUG and c == 0:
                    pl.em("sync", lambda e: e.dma_start(out=dbgS2[:, 0:2], in_=s2c_[:]), waits=[(VE, cv2)], inc=SY, inck=16)
                    pl.em("sync", lambda e: e.dma_start(out=dbgS2[:, 2:4], in_=q2c_[:]), inc=SY, inck=16)
                    pl.em("sync", lambda e: e.dma_start(out=dbgS2[:, 4:6], in_=rstd2[:]), inc=SY, inck=16)
                    pl.em("sync", lambda e: e.dma_start(out=dbgS2[:, 6:8], in_=nmr2[:]), inc=SY, inck=16)
                    pl.em("sync", lambda e: e.dma_start(out=dbgA1, in_=asb[1][:]), inc=SY, inck=16)
                x1vs = []
                for j in range(2):
                    t = tiles[j]
                    sv = pl.em("scalar", lambda e, j=j: e.activation(asb[j][:], asb[j][:], AF.Identity,
                                                                     bias=nmr2[:, j:j + 1], scale=rstd2[:, j:j + 1]),
                               waits=[(VE, cv2)], inc=SC)
                    x1v = pl.em("vector", lambda e, j=j, b=t % 4: e.tensor_add(x1t[j][:], asb[j][:], xin[b][:]),
                                waits=[(SC, sv)], inc=VE)
                    ev[("xin_free", t)] = (VE, x1v)
                    XSPj = dmac[f"XSP{j}"]
                    dv = pl.em("sync", lambda e, j=j, t=t: e.dma_start(out=x1_sp[t], in_=x1t[j][:]),
                               waits=[(VE, x1v)], inc=XSPj, inck=16)
                    ev[("x1sp", t)] = (XSPj, dv)
                    ev[("x1t_spill_prev", c, j)] = (XSPj, dv)
                    x1vs.append(x1v)
                cv3 = bn_chain([x1t[0], x1t[1]], 2, rstd3[:], nmr3[:],
                               [[(VE, x1vs[0])], [(VE, x1vs[1])]])
                for j in range(2):
                    t = tiles[j]
                    sv = pl.em("scalar", lambda e, j=j: e.activation(x1t[j][:], x1t[j][:], AF.Identity,
                                                                     bias=nmr3[:, j:j + 1], scale=rstd3[:, j:j + 1]),
                               waits=[(VE, cv3), ev[("x1sp", t)]], inc=SC)
                    VSPj = dmac[f"VSP{j}"]
                    dv = pl.em("sync", lambda e, j=j, t=t: e.dma_start(out=v_sp[t], in_=x1t[j][:]),
                               waits=[(SC, sv)], inc=VSPj, inck=16)
                    ev[("vsp", t)] = (VSPj, dv)
                    ev[("x1t_spill_prev", c, j)] = (VSPj, dv)
                    pl.em("vector", lambda e, j=j: e.reduce_max(rmaxc[:], x1t[j][:], axis=AX), waits=[(SC, sv)])
                    pl.vdrain()
                    vv = pl.em("vector", lambda e: e.tensor_max(vmaxc[:], vmaxc[:], rmaxc[:]), inc=VE)
                    ev[("asb_free", c, j)] = (VE, vv)
                    ev[("asb_free_sp", c, j)] = ev[("vsp", t)]
                ev[("vmax", c)] = (VE, vv)

                pl.phase = "wprep"
                for _ in range(per_chunk):
                    f = next(wstep_iter, None)
                    if f is not None:
                        f()

            for f in wstep_iter:
                f()

            attn_gate_vals = [(cc_, cc_.n) for cc_ in pl.all_ctrs if cc_ is not GE]

        pl.phase = "ar1"
        gv = pl.em("gpsimd", lambda e: e.dma_start(out=vmx_hbm, in_=vmaxc[:]),
                   waits=[ev[("vmax", NCH - 1)]], inc=GD, inck=16)
        gv = pl.em("gpsimd", lambda e: e.dma_start(out=vrow[:], in_=vmx_hbm.rearrange("p o -> o p")),
                   waits=[(GD, gv)], inc=GD, inck=16)
        vv = pl.em("vector", lambda e: e.memset(scrow[:], -3.0e38), waits=[(GD, gv)])
        pl.vdrain()
        vv = pl.em("vector", lambda e: e.reduce_max(scrow[:, 0:1], vrow[:], axis=AX), inc=VE)
        gv = pl.em("gpsimd", lambda e: e.dma_start(out=cc1i, in_=scrow[:]), waits=[(VE, vv)], inc=GD, inck=16)
        cv = pl.em("gpsimd", lambda e: e.collective_compute(
            "AllReduce", OP.max, replica_groups=[list(range(N_CORES))], ins=[cc1i], outs=[cc1o]),
            waits=[(GD, gv)], inc=GE)
        gv = pl.em("gpsimd", lambda e: e.dma_start(out=grow[:], in_=cc1o), waits=[(GE, cv)], inc=GD, inck=16)
        if _DEBUG:
            pl.em("gpsimd", lambda e: e.dma_start(out=dbg1, in_=grow[:]), waits=[(GD, gv)], inc=GD, inck=16)
        vv = pl.em("vector", lambda e: e.tensor_scalar(row2[:, 0:1], grow[:, 0:1], QEPS, 1.0 / 255.0, OP.max, OP.mult),
                   inc=VE, waits=[(GD, gv)])
        pl.em("scalar", lambda e: e.activation(grow[:, 1:2], row2[:, 0:1], AF.Ln, bias=zerocol[0:1, :], scale=onecol[0:1, :]),
              waits=[(VE, vv)])
        pl.sdrain()
        sv = pl.em("scalar", lambda e: e.activation(row2[:, 1:2], grow[:, 1:2], AF.Exp, bias=zerocol[0:1, :], scale=nonecol[0:1, :]),
                   inc=SC)
        tv = pl.em("tensor", lambda e: e.matmul(PF[4][:, 0:2], onesrow[:], row2[:], start=True, stop=True),
                   waits=[(VE, vv), (SC, sv)], inc=TE)
        tv = pl.tfence(TE)
        vv = pl.em("vector", lambda e: e.tensor_copy(s1col[:], PF[4][:, 0:1]), waits=[(TE, tv)])
        vv = pl.em("vector", lambda e: e.tensor_copy(rs1col[:], PF[4][:, 1:2]), inc=VE)
        ev["s1"] = (VE, vv)
        vv = pl.em("vector", lambda e: e.tensor_scalar(ss1c[:], sw1c[:], s1col[:], None, OP.mult), waits=[ev["s1"]])
        vv = pl.em("vector", lambda e: e.tensor_scalar(bq1c[:], rsw1c[:], rs1col[:], None, OP.mult), waits=[ev["scales"]])
        pl.vdrain()
        vv = pl.em("vector", lambda e: e.tensor_mul(bq1c[:], b1c[:], bq1c[:]), waits=[ev["b1"]])
        pl.vdrain()
        vv = pl.em("vector", lambda e: e.tensor_scalar(bq1c[:], bq1c[:], 1.0, MAGIC, OP.mult, OP.add), inc=VE)
        sv = pl.em("scalar", lambda e: e.activation(bq1c[:], bq1c[:], AF.Identity, bias=nmagcol[:], scale=onecol[:]),
                   waits=[(VE, vv)], inc=SC)
        vv = pl.em("vector", lambda e: e.tensor_mul(bq1c[:], bq1c[:], ss1c[:]), waits=[(SC, sv)], inc=VE)
        ev["bq1"] = (VE, vv)
        if _DEBUG:
            pl.em("sync", lambda e: e.dma_start(out=dbgFF[:, 0:32], in_=ss1c[:]), waits=[ev["bq1"]], inc=SY, inck=16)
            pl.em("sync", lambda e: e.dma_start(out=dbgFF[:, 32:64], in_=bq1c[:]), inc=SY, inck=16)
            pl.em("sync", lambda e: e.dma_start(out=dbgFF[:, 64:65], in_=s1col[:], allow_slow_non_contiguous=True), inc=SY, inck=16)
            pl.em("sync", lambda e: e.dma_start(out=dbgFF[:, 65:66], in_=rs1col[:], allow_slow_non_contiguous=True), inc=SY, inck=16)
            pl.em("sync", lambda e: e.dma_start(out=dbgFF[:, 66:67], in_=sw1c[:, 0:1], allow_slow_non_contiguous=True), inc=SY, inck=16)
            pl.em("sync", lambda e: e.dma_start(out=dbgFF[:, 67:68], in_=rsw1c[:, 0:1], allow_slow_non_contiguous=True), inc=SY, inck=16)

        attn_gate = attn_gate_vals
        with ExitStack() as fctx:
            sbf = lambda n, s, d: fctx.enter_context(nc.sbuf_tensor(n, s, d))
            w2i = sbf("w2i", [P, 32 * D], BF16)
            srow_t = sbf("srow_t", [P, D], F32)
            brow_t = sbf("brow_t", [P, D], F32)
            row1024a = sbf("row1024a", [1, D], F32)
            row1024b = sbf("row1024b", [1, D], F32)

            w2i_waits = list(attn_gate) + [ev[("w2isp", 1, 31)]]
            for dc8 in range(8):
                gv = pl.em("gpsimd",
                           lambda e, dc8=dc8: e.dma_start(
                               out=w2i[:, dc8 * 4096:(dc8 + 1) * 4096].rearrange("p (f e) -> p f e", e=D),
                               in_=w2i_sp[dc8 * 4:(dc8 + 1) * 4].rearrange("f p e -> p f e")),
                           waits=w2i_waits if dc8 == 0 else [], inc=dmac["W2L"], inck=16)
            ev["w2i"] = (dmac["W2L"], gv)

            with ExitStack() as f1ctx:
                sb1 = lambda n, s, d: f1ctx.enter_context(nc.sbuf_tensor(n, s, d))
                w1i = sb1("w1i", [P, 8 * FF], BF16)
                vload = [sb1(f"vload{i}", [P, D], F32) for i in range(2)]
                vtmp = sb1("vtmp", [P, D], F32)
                vqtm = sb1("vqtm", [P, D], BF16)
                vqfm = sb1("vqfm", [P, 8 * 512], BF16)
                hev = [sb1(f"hev{i}", [P, 512], F32) for i in range(2)]

                w1i_waits = list(attn_gate) + [ev[("w1isp", 7, 7)]]
                for dc in range(8):
                    gv = pl.em("gpsimd",
                               lambda e, dc=dc: e.dma_start(out=w1i[:, dc * FF:(dc + 1) * FF], in_=w1i_sp[dc]),
                               waits=w1i_waits if dc == 0 else [ev[("w1isp", 7, dc)]], inc=dmac["W1L"], inck=16)
                ev["w1i"] = (dmac["W1L"], gv)

                pl.phase = "ffn1"
                for tc in range(8):
                    for jj in range(4):
                        t = tc * 4 + jj
                        b = t % 2
                        lw = [ev[("vsp", t)], ev.get(("vload_free", t - 2), (VE, 0))] + (attn_gate if t == 0 else [])
                        VLb = dmac[f"VL{b}"]
                        dv = pl.em("scalar", lambda e, b=b, t=t: e.dma_start(out=vload[b][:], in_=v_sp[t]),
                                   waits=lw, inc=VLb, inck=16)
                        vv = pl.em("vector", lambda e, b=b: e.tensor_scalar(vtmp[:], vload[b][:], rs1col[:], MAGIC,
                                                                            OP.mult, OP.add),
                                   waits=[(VLb, dv), ev["s1"], ev.get(("vtmp_free", t - 1), (SC, 0))], inc=VE)
                        ev[("vload_free", t)] = (VE, vv)
                        sv = pl.em("scalar", lambda e: e.activation(vqtm[:], vtmp[:], AF.Relu,
                                                                    bias=nmagcol[:], scale=onecol[:]),
                                   waits=[(VE, vv), ev.get(("vqtm_free", t - 1), (TE, 0))], inc=SC)
                        ev[("vtmp_free", t)] = (SC, sv)
                        qv = trans8(vqtm,
                                    lambda e, pb, jj=jj: e.tensor_copy(
                                        vqfm[:].rearrange("p (c w t) -> p c w t", c=8, w=4)[:, :, jj],
                                        pb[:].rearrange("p (c t) -> p c t", c=8)),
                                    (SC, sv),
                                    extra_waits=[ev.get(("vqfm_free", tc - 1), (TE, 0))])
                        ev[("vqtm_free", t)] = (TE, TE.n)
                        ev[("vqfm", t)] = (VE, qv)
                        if _DEBUG and t == 0:
                            pl.em("sync", lambda e: e.dma_start(out=dbgVQ, in_=vqfm[:, 0:512]),
                                  waits=[(VE, qv)], inc=SY, inck=16)
                    for fc in range(32):
                        hb = fc % 2
                        for dc in range(8):
                            waits = []
                            if dc == 0:
                                waits = [(VE, ev[("vqfm", tc * 4 + 3)][1]), ev["w1i"],
                                         ev.get(("hev_free", tc * 32 + fc - 2), (VE, 0)),
                                         ev.get(("hev_free2", tc * 32 + fc - 2), (VE, 0))]
                            tv = pl.em("tensor",
                                       lambda e, hb=hb, dc=dc, fc=fc: e.matmul(
                                           PF[hb][:], w1i[:, dc * FF + fc * 128: dc * FF + (fc + 1) * 128],
                                           vqfm[:, dc * 512:(dc + 1) * 512],
                                           start=(dc == 0), stop=(dc == 7), skip_group_check=True),
                                       waits=waits, inc=TE if dc == 7 else None)
                        ffv = pl.tfence(TE)
                        sv = pl.em("scalar", lambda e, hb=hb, fc=fc: e.activation(
                            hev[hb][:], PF[hb][:], AF.Relu, bias=bq1c[:, fc:fc + 1], scale=ss1c[:, fc:fc + 1]),
                            waits=[(TE, ffv), ev["bq1"]], inc=SC)
                        HSb = dmac[f"HS{hb}"]
                        dv = pl.em("scalar", lambda e, hb=hb, fc=fc, tc=tc: e.dma_start(
                            out=h_sp[fc, :, tc * 512:(tc + 1) * 512], in_=hev[hb][:]),
                            waits=[(SC, sv)], inc=HSb, inck=16)
                        ev[("hsp", fc, tc)] = (HSb, dv)
                        ev[("hev_free", tc * 32 + fc)] = (HSb, dv)
                        pl.em("vector", lambda e, hb=hb: e.reduce_max(rmaxc[:], hev[hb][:], axis=AX),
                              waits=[(SC, sv)])
                        pl.vdrain()
                        vv = pl.em("vector", lambda e: e.tensor_max(hmaxc[:], hmaxc[:], rmaxc[:]), inc=VE)
                        ev[("hev_free2", tc * 32 + fc)] = (VE, vv)
                        ev[("vqfm_free", tc)] = (TE, TE.n)
                ev["ffn1_te"] = (TE, TE.n)
                ev["ffn1_ve"] = (VE, VE.n)
                ev["ffn1_sc"] = (SC, SC.n)
                ev["ffn1_sd"] = (dmac["HS0"], dmac["HS0"].n)
                ev["ffn1_sd2"] = (dmac["HS1"], dmac["HS1"].n)
                ev["hmax"] = (VE, VE.n)

            pl.phase = "ar2"
            gv = pl.em("gpsimd", lambda e: e.dma_start(out=hmx_hbm, in_=hmaxc[:]),
                       waits=[ev["hmax"]], inc=GD, inck=16)
            gv = pl.em("gpsimd", lambda e: e.dma_start(out=vrow[:], in_=hmx_hbm.rearrange("p o -> o p")),
                       waits=[(GD, gv)], inc=GD, inck=16)
            vv = pl.em("vector", lambda e: e.memset(scrow[:], -3.0e38), waits=[(GD, gv)])
            pl.vdrain()
            vv = pl.em("vector", lambda e: e.reduce_max(scrow[:, 0:1], vrow[:], axis=AX), inc=VE)
            gv = pl.em("gpsimd", lambda e: e.dma_start(out=cc2i, in_=scrow[:]), waits=[(VE, vv)], inc=GD, inck=16)
            cv = pl.em("gpsimd", lambda e: e.collective_compute(
                "AllReduce", OP.max, replica_groups=[list(range(N_CORES))], ins=[cc2i], outs=[cc2o]),
                waits=[(GD, gv)], inc=GE)
            gv = pl.em("gpsimd", lambda e: e.dma_start(out=grow[:], in_=cc2o), waits=[(GE, cv)], inc=GD, inck=16)
            if _DEBUG:
                pl.em("gpsimd", lambda e: e.dma_start(out=dbg2, in_=grow[:]), waits=[(GD, gv)], inc=GD, inck=16)
            vv = pl.em("vector", lambda e: e.tensor_scalar(row2[:, 0:1], grow[:, 0:1], QEPS, 1.0 / 255.0, OP.max, OP.mult),
                       inc=VE, waits=[(GD, gv)])
            pl.em("scalar", lambda e: e.activation(grow[:, 1:2], row2[:, 0:1], AF.Ln, bias=zerocol[0:1, :], scale=onecol[0:1, :]),
                  waits=[(VE, vv)])
            pl.sdrain()
            sv = pl.em("scalar", lambda e: e.activation(row2[:, 1:2], grow[:, 1:2], AF.Exp, bias=zerocol[0:1, :], scale=nonecol[0:1, :]),
                       inc=SC)
            tv = pl.em("tensor", lambda e: e.matmul(PF[4][:, 0:2], onesrow[:], row2[:], start=True, stop=True),
                       waits=[(VE, vv), (SC, sv), (TE, ev["ffn1_te"][1])], inc=TE)
            tv = pl.tfence(TE)
            vv = pl.em("vector", lambda e: e.tensor_copy(s2col[:], PF[4][:, 0:1]), waits=[(TE, tv)])
            vv = pl.em("vector", lambda e: e.tensor_copy(rs2col[:], PF[4][:, 1:2]), inc=VE)
            ev["s2"] = (VE, vv)
            vv = pl.em("vector", lambda e: e.tensor_scalar(ss2c[:], sw2c[:], s2col[:], None, OP.mult), waits=[ev["s2"]])
            vv = pl.em("vector", lambda e: e.tensor_scalar(bq2c[:], rsw2c[:], rs2col[:], None, OP.mult), waits=[ev["scales"]])
            pl.vdrain()
            vv = pl.em("vector", lambda e: e.tensor_mul(bq2c[:], b2c[:], bq2c[:]), waits=[ev["b2"]])
            pl.vdrain()
            vv = pl.em("vector", lambda e: e.tensor_scalar(bq2c[:], bq2c[:], 1.0, MAGIC, OP.mult, OP.add), inc=VE)
            sv = pl.em("scalar", lambda e: e.activation(bq2c[:], bq2c[:], AF.Identity, bias=nmagcol[:], scale=onecol[:]),
                       waits=[(VE, vv), (SC, ev["ffn1_sc"][1])], inc=SC)
            vv = pl.em("vector", lambda e: e.tensor_mul(bq2c[:], bq2c[:], ss2c[:]), waits=[(SC, sv)], inc=VE)
            ev["bq2"] = (VE, vv)
            tv = pl.em("tensor", lambda e: e.transpose(PF[4][:8, :P], ss2c[:], idf[:]),
                       waits=[ev["bq2"]], inc=TE)
            tv = pl.tfence(TE)
            vv = pl.em("vector", lambda e: e.tensor_copy(rowst[:], PF[4][:8, :P]),
                       waits=[(TE, tv), ev.get("rowst_free", (VE, 0))], inc=VE)
            for cc in range(8):
                gv = pl.em("gpsimd",
                           lambda e, cc=cc: e.dma_start(out=row1024a[:, cc * P:(cc + 1) * P], in_=rowst[cc:cc + 1, :]),
                           waits=[(VE, vv)] if cc == 0 else [], inc=GD, inck=16)
            ev["rowst_free"] = (GD, gv)
            tv = pl.em("tensor", lambda e: e.transpose(PF[5][:8, :P], bq2c[:], idf[:]),
                       waits=[(VE, vv)], inc=TE)
            tv = pl.tfence(TE)
            vv = pl.em("vector", lambda e: e.tensor_copy(rowst[:], PF[5][:8, :P]),
                       waits=[(TE, tv), ev.get("rowst_free", (VE, 0))], inc=VE)
            for cc in range(8):
                gv2 = pl.em("gpsimd",
                            lambda e, cc=cc: e.dma_start(out=row1024b[:, cc * P:(cc + 1) * P], in_=rowst[cc:cc + 1, :]),
                            waits=[(VE, vv)] if cc == 0 else [], inc=GD, inck=16)
            ev["rowst_free"] = (GD, gv2)
            t1 = pl.em("tensor", lambda e: e.matmul(PF[0][:], onesrow[:], row1024a[:, :512], start=True, stop=True),
                       waits=[(GD, gv), (SC, ev["ffn1_sc"][1]), (VE, ev["ffn1_ve"][1])])
            t1 = pl.em("tensor", lambda e: e.matmul(PF[1][:], onesrow[:], row1024a[:, 512:], start=True, stop=True))
            t1 = pl.em("tensor", lambda e: e.matmul(PF[2][:], onesrow[:], row1024b[:, :512], start=True, stop=True),
                       waits=[(GD, gv2)])
            t1 = pl.em("tensor", lambda e: e.matmul(PF[3][:], onesrow[:], row1024b[:, 512:], start=True, stop=True),
                       inc=TE)
            t1 = pl.tfence(TE)
            v1 = pl.em("vector", lambda e: e.tensor_copy(srow_t[:, :512], PF[0][:]), waits=[(TE, t1)])
            v1 = pl.em("vector", lambda e: e.tensor_copy(srow_t[:, 512:], PF[1][:]))
            v1 = pl.em("vector", lambda e: e.tensor_copy(brow_t[:, :512], PF[2][:]))
            v1 = pl.em("vector", lambda e: e.tensor_copy(brow_t[:, 512:], PF[3][:]), inc=VE)
            ev["rows"] = (VE, v1)
            if _DEBUG:
                pl.em("sync", lambda e: e.dma_start(out=dbgSR, in_=srow_t[:]), waits=[ev["rows"]], inc=SY, inck=16)
                pl.em("sync", lambda e: e.dma_start(out=dbgBR, in_=brow_t[:]), inc=SY, inck=16)

            ffn1_gate = [(TE, ev["ffn1_te"][1]), (VE, ev["ffn1_ve"][1]), (SC, ev["ffn1_sc"][1]),
                         ev["ffn1_sd"], ev["ffn1_sd2"]]
            with ExitStack() as f2ctx:
                sb2 = lambda n, s, d: f2ctx.enter_context(nc.sbuf_tensor(n, s, d))
                hq = [sb2(f"hq{i}", [P, 32 * 512], BF16) for i in range(2)]
                hload = [sb2(f"hload{i}", [P, 512], F32) for i in range(4)]
                htmp = [sb2(f"htmp{i}", [P, 512], F32) for i in range(2)]
                out2 = [sb2(f"out2{i}", [P, D], F32) for i in range(4)]
                sqo = sb2("sqo", [P, D], F32)
                x1re = [sb2(f"x1re{i}", [P, D], F32) for i in range(4)]
                outt = [sb2(f"outt{i}", [P, D], F32) for i in range(2)]
                s4c_ = sb2("s4c_", [P, 4], F32)
                q4c_ = sb2("q4c_", [P, 4], F32)
                rstd4 = sb2("rstd4", [P, 4], F32)
                nmr4 = sb2("nmr4", [P, 4], F32)

                pl.phase = "ffn2"
                for tw in range(8):
                    wb_ = tw % 2
                    for fc in range(32):
                        b = fc % 4
                        lw = [ev[("hsp", fc, tw)], ev.get(("hload_free", tw * 32 + fc - 4), (VE, 0))]
                        if tw == 0 and fc == 0:
                            lw += ffn1_gate
                        HLb = dmac[f"HL{b}"]
                        dv = pl.em("sync", lambda e, b=b, fc=fc, tw=tw: e.dma_start(
                            out=hload[b][:], in_=h_sp[fc, :, tw * 512:(tw + 1) * 512]),
                            waits=lw, inc=HLb, inck=16)
                        vv = pl.em("vector", lambda e, b=b: e.tensor_scalar(htmp[b % 2][:], hload[b][:], rs2col[:], MAGIC,
                                                                            OP.mult, OP.add),
                                   waits=[(HLb, dv), ev["s2"], ev.get(("htmp_free", tw * 32 + fc - 2), (SC, 0))], inc=VE)
                        ev[("hload_free", tw * 32 + fc)] = (VE, vv)
                        sv = pl.em("scalar", lambda e, b=b, fc=fc, wb_=wb_: e.activation(
                            hq[wb_][:, fc * 512:(fc + 1) * 512], htmp[b % 2][:], AF.Identity,
                            bias=nmagcol[:], scale=onecol[:]),
                            waits=[(VE, vv), ev.get(("hq_free", tw - 2), (TE, 0))], inc=SC)
                        ev[("htmp_free", tw * 32 + fc)] = (SC, sv)
                        ev[("hqfc", tw, fc)] = (SC, sv)
                    ev[("hq", tw)] = (SC, SC.n)
                    if _DEBUG and tw == 0:
                        pl.em("sync", lambda e, wb_=wb_: e.dma_start(out=dbgHQ, in_=hq[wb_][:, 0:512]),
                              waits=[ev[("hq", 0)]], inc=SY, inck=16)
                    ln4ws = []
                    for tt in range(4):
                        t = tw * 4 + tt
                        tb = tt % 2
                        XRt = dmac[f"XR{tt}"]
                        rv = pl.em("sync", lambda e, t=t, tt=tt: e.dma_start(out=x1re[tt][:], in_=x1_sp[t]),
                                   waits=[ev[("x1sp", t)], ev.get(("x1re_free", t - 4), (VE, 0))] +
                                         (attn_gate + ffn1_gate if t == 0 else []),
                                   inc=XRt, inck=16)
                        ev[("x1re", t)] = (XRt, rv)
                        pA = PF[0 + 2 * tb]
                        pB = PF[1 + 2 * tb]
                        for fc in range(32):
                            waits = [ev[("hqfc", tw, fc)]]
                            if fc == 0:
                                waits += [ev["w2i"],
                                          ev.get(("psO_free", t - 2), (VE, 0)), ev["rows"]]
                            pl.em("tensor", lambda e, pA=pA, fc=fc, tt=tt, wb_=wb_: e.matmul(
                                pA[:], hq[wb_][:, fc * 512 + tt * 128: fc * 512 + (tt + 1) * 128],
                                w2i[:, fc * D: fc * D + 512],
                                start=(fc == 0), stop=(fc == 31), skip_group_check=True),
                                waits=waits)
                            tv = pl.em("tensor", lambda e, pB=pB, fc=fc, tt=tt, wb_=wb_: e.matmul(
                                pB[:], hq[wb_][:, fc * 512 + tt * 128: fc * 512 + (tt + 1) * 128],
                                w2i[:, fc * D + 512: (fc + 1) * D],
                                start=(fc == 0), stop=(fc == 31), skip_group_check=True),
                                inc=TE if fc == 31 else None)
                        ev[("o2mm", t)] = (TE, tv)
                        if tt == 3:
                            ev[("hq_free", tw)] = (TE, tv)
                        ofv = pl.tfence(TE)
                        ovw = [(TE, ofv), ev.get(("out2_free", tw - 1, tt), (SY, 0))]
                        pl.em("vector", lambda e, pA=pA, tt=tt: e.tensor_mul(out2[tt][:, :512], pA[:], srow_t[:, :512]),
                              waits=ovw)
                        pl.em("vector", lambda e, pB=pB, tt=tt: e.tensor_mul(out2[tt][:, 512:], pB[:], srow_t[:, 512:]))
                        pl.vdrain()
                        pl.em("vector", lambda e, tt=tt: e.tensor_add(out2[tt][:, :512], out2[tt][:, :512], brow_t[:, :512]))
                        ov = pl.em("vector", lambda e, tt=tt: e.tensor_add(out2[tt][:, 512:], out2[tt][:, 512:], brow_t[:, 512:]),
                                   inc=VE)
                        ev[("psO_free", t)] = (VE, ov)
                        ln4ws.append([(VE, ov)])
                    cv4 = bn_chain([out2[0], out2[1], out2[2], out2[3]], 4,
                                   rstd4[:], nmr4[:], ln4ws)
                    if _DEBUG and tw == 0:
                        pl.em("sync", lambda e: e.dma_start(out=dbgO2, in_=out2[0][:]), waits=[(VE, cv4)], inc=SY, inck=16)
                        pl.em("sync", lambda e: e.dma_start(out=dbgS4[:, 0:4], in_=s4c_[:]), inc=SY, inck=16)
                        pl.em("sync", lambda e: e.dma_start(out=dbgS4[:, 4:8], in_=q4c_[:]), inc=SY, inck=16)
                        pl.em("sync", lambda e: e.dma_start(out=dbgS4[:, 8:12], in_=rstd4[:]), inc=SY, inck=16)
                        pl.em("sync", lambda e: e.dma_start(out=dbgS4[:, 12:16], in_=nmr4[:]), inc=SY, inck=16)
                    for tt in range(4):
                        t = tw * 4 + tt
                        tb = tt % 2
                        sv = pl.em("scalar", lambda e, tt=tt: e.activation(out2[tt][:], out2[tt][:], AF.Identity,
                                                                           bias=nmr4[:, tt:tt + 1], scale=rstd4[:, tt:tt + 1]),
                                   waits=[(VE, cv4)], inc=SC)
                        ov = pl.em("vector", lambda e, tb=tb, tt=tt: e.tensor_add(outt[tb][:], out2[tt][:], x1re[tt][:]),
                                   waits=[(SC, sv), ev[("x1re", t)],
                                          ev.get(("outt_free", t - 2), (VE, 0))], inc=VE)
                        ev[("x1re_free", t)] = (VE, ov)
                        ev[("out2_free", tw, tt)] = (VE, ov)
                        OTb = dmac[f"OT{tb}"]
                        dv = pl.em("sync", lambda e, tb=tb, t=t: e.dma_start(out=out_hbm[t * P:(t + 1) * P, :], in_=outt[tb][:]),
                                   waits=[(VE, ov)], inc=OTb, inck=16)
                        ev[("outt_free", t)] = (OTb, dv)

                pl.wait_only("sync", [(dmac["OT0"], dmac["OT0"].n), (dmac["OT1"], dmac["OT1"].n)])

        with nc.Block() as block:
            @block.sync
            def _(eng):
                for s in pl.steps["sync"]:
                    s(eng)

            @block.scalar
            def _(eng):
                for s in pl.steps["scalar"]:
                    s(eng)

            @block.vector
            def _(eng):
                for s in pl.steps["vector"]:
                    s(eng)

            @block.tensor
            def _(eng):
                for s in pl.steps["tensor"]:
                    s(eng)

            @block.gpsimd
            def _(eng):
                for s in pl.steps["gpsimd"]:
                    s(eng)

    return nc


_CACHE = {}


def kernel(L0_attn_X, L0_attn_W_Q, L0_attn_W_K, L0_attn_W_V,
           L0_ffn_w1, L0_ffn_b1, L0_ffn_w2, L0_ffn_b2):
    key = ("nc", _DEBUG, repr(sorted(_SERIALIZE)) if isinstance(_SERIALIZE, set) else _SERIALIZE)
    if key not in _CACHE:
        _CACHE[key] = _build()
    if "const" not in _CACHE:
        _CACHE["const"] = _host_constants()
    nc = _CACHE[key]
    cst = _CACHE["const"]

    X = np.asarray(L0_attn_X, dtype=np.float32)
    shared = {
        "wq": np.ascontiguousarray(np.asarray(L0_attn_W_Q, np.float32)),
        "wk": np.ascontiguousarray(np.asarray(L0_attn_W_K, np.float32)),
        "wv": np.ascontiguousarray(np.asarray(L0_attn_W_V, np.float32)),
        "w1n": np.ascontiguousarray(np.asarray(L0_ffn_w1, np.float32)),
        "w1t": np.ascontiguousarray(np.asarray(L0_ffn_w1, np.float32).T),
        "w2n": np.ascontiguousarray(np.asarray(L0_ffn_w2, np.float32)),
        "w2t": np.ascontiguousarray(np.asarray(L0_ffn_w2, np.float32).T),
        "b1c": np.ascontiguousarray(np.asarray(L0_ffn_b1, np.float32).reshape(32, P).T),
        "b2c": np.ascontiguousarray(np.asarray(L0_ffn_b2, np.float32).reshape(8, P).T),
        **cst,
    }
    in_maps = [dict(shared, x=np.ascontiguousarray(X[b])) for b in range(N_CORES)]
    res = run_bass_kernel_spmd(nc, in_maps, core_ids=list(range(N_CORES)))
    if _DEBUG:
        _CACHE["dbg"] = res.results
    return np.stack([res.results[b]["out"] for b in range(N_CORES)], axis=0)


if __name__ == "__main__":
    nc = _build()
    print("build ok")


# version 15
# speedup vs baseline: 1.2027x; 1.2027x over previous
import numpy as np
import ml_dtypes
from contextlib import ExitStack

import concourse.bass as bass
import concourse.mybir as mybir
from concourse.bass_utils import run_bass_kernel_spmd

F32 = mybir.dt.float32
BF16 = mybir.dt.bfloat16
AX = mybir.AxisListType.X
OP = mybir.AluOpType
AF = mybir.ActivationFunctionType

P = 128
T = 4096
D = 1024
FF = 4096
NT = T // P
NCH = NT // 2
GAMMA = 0.96875
MAGIC = float(np.float32(3 * 2.0 ** 22))
LNEPS = 1e-5
QEPS = 1e-8
N_CORES = 8


def _gp(k):
    with np.errstate(under="ignore"):
        return float(np.float32(np.float64(GAMMA) ** k))


def _host_constants():
    t = np.arange(T, dtype=np.float64)[:, None]
    i = np.arange(D // 2, dtype=np.float64)[None, :]
    sv = (2.0 * i + 0.4 * D) / (1.4 * D)
    scale = sv ** (t / 512.0)
    inv_freq = 1.0 / 10000.0 ** (i / (D // 2))
    sinus = t * inv_freq
    sin_s = np.sin(sinus)
    cos_s = np.cos(sinus)

    def rep2(a):
        return np.repeat(a, 2, axis=1)

    gq = np.float64(GAMMA) ** (np.arange(T) % 256).astype(np.float64)
    gk = np.float64(GAMMA) ** (-(np.arange(T) % 128)).astype(np.float64)
    sgn = np.where(np.arange(D) % 2 == 0, -1.0, 1.0)[None, :]

    fcq = rep2(cos_s * scale) * gq[:, None]
    fsq = rep2(sin_s * scale) * gq[:, None] * sgn
    fck = rep2(cos_s / scale) * gk[:, None]
    fsk = rep2(sin_s / scale) * gk[:, None] * sgn

    tri = (np.arange(P)[None, :] >= np.arange(P)[:, None]).astype(np.float32)
    return {
        "fcq": fcq.astype(np.float32), "fsq": fsq.astype(np.float32),
        "fck": fck.astype(np.float32), "fsk": fsk.astype(np.float32),
        "idbf": np.eye(P, dtype=ml_dtypes.bfloat16),
        "idf32": np.eye(P, dtype=np.float32),
        "tri": tri,
    }


class _Ctr:

    def __init__(self, ctx, nc, name):
        self.h = ctx.enter_context(nc.semaphore(name))
        self.n = 0
        self.name = name


_SERIALIZE = False


class _Plan:

    ENGINES = ("sync", "scalar", "vector", "tensor", "gpsimd")

    def __init__(self, all_ctrs=None):
        self.steps = {e: [] for e in self.ENGINES}
        self.waited = {e: {} for e in self.ENGINES}
        self.all_ctrs = all_ctrs or []
        self.phase = "prolog"

    def em(self, eng, fn, waits=(), inc=None, inck=1):
        if _SERIALIZE and (_SERIALIZE is True or eng in _SERIALIZE or self.phase in _SERIALIZE):
            waits = list(waits) + [(c, c.n) for c in self.all_ctrs]
        ws = []
        for c, v in waits:
            if v <= 0:
                continue
            assert v <= c.n, f"wait on {c.name} for {v} but only {c.n} planned"
            prev = self.waited[eng].get(c.name, 0)
            if v > prev:
                ws.append((c, v))
                self.waited[eng][c.name] = v
        ih = None
        if inc is not None:
            inc.n += inck
            ih = (inc.h, inck)
        val = inc.n if inc is not None else None
        drain_inc = eng in ("vector", "scalar") and inck == 1

        def step(e, _ws=tuple(ws), _fn=fn, _ih=ih, _dr=drain_inc):
            for c, v in _ws:
                e.wait_ge(c.h, v)
            ins = _fn(e)
            if _ih is not None:
                if _dr:
                    e.drain().then_inc(_ih[0], _ih[1])
                else:
                    ins.then_inc(_ih[0], _ih[1])

        self.steps[eng].append(step)
        return val

    def vdrain(self):
        self.steps["vector"].append(lambda e: e.drain())

    def sdrain(self):
        self.steps["scalar"].append(lambda e: e.drain())

    def tfence(self, TE):
        TE.n += 1
        v = TE.n
        self.steps["tensor"].append(lambda e, _v=(TE.h,): e.drain().then_inc(TE.h, 1))
        return v

    def wait_only(self, eng, waits):
        ws = []
        for c, v in waits:
            if v <= 0:
                continue
            prev = self.waited[eng].get(c.name, 0)
            if v > prev:
                ws.append((c, v))
                self.waited[eng][c.name] = v
        if not ws:
            return

        def step(e, _ws=tuple(ws)):
            for c, v in _ws:
                e.wait_ge(c.h, v)

        self.steps[eng].append(step)


_DEBUG = False


def _build():
    nc = bass.Bass("TRN2", target_bir_lowering=False)
    dbg_kind = "ExternalOutput" if _DEBUG else "Internal"

    ap = lambda t: t.ap()
    x_in = ap(nc.dram_tensor("x", [T, D], F32, kind="ExternalInput"))
    wq_in = ap(nc.dram_tensor("wq", [D, D], F32, kind="ExternalInput"))
    wk_in = ap(nc.dram_tensor("wk", [D, D], F32, kind="ExternalInput"))
    wv_in = ap(nc.dram_tensor("wv", [D, D], F32, kind="ExternalInput"))
    w1n_in = ap(nc.dram_tensor("w1n", [FF, D], F32, kind="ExternalInput"))
    w1t_in = ap(nc.dram_tensor("w1t", [D, FF], F32, kind="ExternalInput"))
    w2n_in = ap(nc.dram_tensor("w2n", [D, FF], F32, kind="ExternalInput"))
    w2t_in = ap(nc.dram_tensor("w2t", [FF, D], F32, kind="ExternalInput"))
    b1_in = ap(nc.dram_tensor("b1c", [P, FF // P], F32, kind="ExternalInput"))
    b2_in = ap(nc.dram_tensor("b2c", [P, D // P], F32, kind="ExternalInput"))
    fcq_in = ap(nc.dram_tensor("fcq", [T, D], F32, kind="ExternalInput"))
    fsq_in = ap(nc.dram_tensor("fsq", [T, D], F32, kind="ExternalInput"))
    fck_in = ap(nc.dram_tensor("fck", [T, D], F32, kind="ExternalInput"))
    fsk_in = ap(nc.dram_tensor("fsk", [T, D], F32, kind="ExternalInput"))
    idbf_in = ap(nc.dram_tensor("idbf", [P, P], BF16, kind="ExternalInput"))
    idf_in = ap(nc.dram_tensor("idf32", [P, P], F32, kind="ExternalInput"))
    tri_in = ap(nc.dram_tensor("tri", [P, P], F32, kind="ExternalInput"))
    out_hbm = ap(nc.dram_tensor("out", [T, D], F32, kind="ExternalOutput"))

    k_sp = ap(nc.dram_tensor("k_sp", [NT, P, D], BF16, kind=dbg_kind))
    x1_sp = ap(nc.dram_tensor("x1_sp", [NT, P, D], F32, kind=dbg_kind))
    v_sp = ap(nc.dram_tensor("v_sp", [NT, P, D], F32, kind=dbg_kind))
    h_sp = ap(nc.dram_tensor("h_sp", [FF // P, P, T], F32, kind=dbg_kind))
    w1i_sp = ap(nc.dram_tensor("w1i_sp", [D // P, P, FF], BF16, kind=dbg_kind))
    w2i_sp = ap(nc.dram_tensor("w2i_sp", [FF // P, P, D], BF16, kind=dbg_kind))
    vmx_hbm = ap(nc.dram_tensor("vmx_hbm", [P, 1], F32, kind=dbg_kind))
    hmx_hbm = ap(nc.dram_tensor("hmx_hbm", [P, 1], F32, kind=dbg_kind))
    cc1i = ap(nc.dram_tensor("cc1i", [1, 8], F32))
    cc1o = ap(nc.dram_tensor("cc1o", [1, 8], F32))
    cc2i = ap(nc.dram_tensor("cc2i", [1, 8], F32))
    cc2o = ap(nc.dram_tensor("cc2o", [1, 8], F32))
    if _DEBUG:
        dbg1 = ap(nc.dram_tensor("dbg1", [1, 8], F32, kind="ExternalOutput"))
        dbg2 = ap(nc.dram_tensor("dbg2", [1, 8], F32, kind="ExternalOutput"))
        dbgU = ap(nc.dram_tensor("dbgU", [P, D], BF16, kind="ExternalOutput"))
        dbgUFM = ap(nc.dram_tensor("dbgUFM", [P, D], BF16, kind="ExternalOutput"))
        dbgQP = ap(nc.dram_tensor("dbgQP", [P, D], BF16, kind="ExternalOutput"))
        dbgKP = ap(nc.dram_tensor("dbgKP", [P, D], BF16, kind="ExternalOutput"))
        dbgST = ap(nc.dram_tensor("dbgST", [P, 512], BF16, kind="ExternalOutput"))
        dbgA = ap(nc.dram_tensor("dbgA", [P, D], F32, kind="ExternalOutput"))
        dbgSW1 = ap(nc.dram_tensor("dbgSW1", [P, 32], F32, kind="ExternalOutput"))
        dbgRSW1 = ap(nc.dram_tensor("dbgRSW1", [P, 32], F32, kind="ExternalOutput"))
        dbgSW2 = ap(nc.dram_tensor("dbgSW2", [P, 8], F32, kind="ExternalOutput"))
        dbgRT = ap(nc.dram_tensor("dbgRT", [P, 512], F32, kind="ExternalOutput"))
        dbgS0 = ap(nc.dram_tensor("dbgS0", [P, 16], F32, kind="ExternalOutput"))
        dbgS1 = ap(nc.dram_tensor("dbgS1", [P, 16], F32, kind="ExternalOutput"))
        dbgS2 = ap(nc.dram_tensor("dbgS2", [P, 16], F32, kind="ExternalOutput"))
        dbgA1 = ap(nc.dram_tensor("dbgA1", [P, D], F32, kind="ExternalOutput"))
        dbgO2 = ap(nc.dram_tensor("dbgO2", [P, D], F32, kind="ExternalOutput"))
        dbgHQ = ap(nc.dram_tensor("dbgHQ", [P, 512], BF16, kind="ExternalOutput"))
        dbgS4 = ap(nc.dram_tensor("dbgS4", [P, 16], F32, kind="ExternalOutput"))
        dbgSR = ap(nc.dram_tensor("dbgSR", [P, D], F32, kind="ExternalOutput"))
        dbgBR = ap(nc.dram_tensor("dbgBR", [P, D], F32, kind="ExternalOutput"))
        dbgVQ = ap(nc.dram_tensor("dbgVQ", [P, 512], BF16, kind="ExternalOutput"))
        dbgFF = ap(nc.dram_tensor("dbgFF", [P, 68], F32, kind="ExternalOutput"))

    pl = _Plan()
    ev = {}

    with ExitStack() as gctx:
        sbg = lambda n, s, d: gctx.enter_context(nc.sbuf_tensor(n, s, d))
        psg = lambda n, s, d: gctx.enter_context(nc.psum_tensor(n, s, d))

        TE = _Ctr(gctx, nc, "TE")
        VE = _Ctr(gctx, nc, "VE")
        SC = _Ctr(gctx, nc, "SC")
        SY = _Ctr(gctx, nc, "SY")
        SD = _Ctr(gctx, nc, "SD")
        GD = _Ctr(gctx, nc, "GD")
        GE = _Ctr(gctx, nc, "GE")
        _dn = ["C0", "XI0", "XI1", "XI2", "XI3", "FL0", "FL1", "KSP", "XSP0", "XSP1", "VSP0", "VSP1",
               "KL0", "KL1", "KL2", "KL3", "WG0", "WG1", "WS", "RR", "VL0", "VL1", "HS0", "HS1",
               "HL0", "HL1", "HL2", "HL3", "XR0", "XR1", "XR2", "XR3", "OT0", "OT1", "W1L", "W2L"]
        dmac = {n: _Ctr(gctx, nc, n) for n in _dn}
        pl.all_ctrs = [TE, VE, SC, SY, SD, GD, GE] + list(dmac.values())

        PF = [psg(f"pf{i}", [P, 512], F32) for i in range(6)]
        PB = [psg(f"pb{i}", [P, 1024], BF16) for i in range(2)]

        idbf = sbg("idbf_t", [P, P], BF16)
        idf = sbg("idf_t", [P, P], F32)
        trit = sbg("tri_t", [P, P], F32)
        onesrow = sbg("onesrow", [1, P], F32)
        onecol = sbg("onecol", [P, 1], F32)
        epscol = sbg("epscol", [P, 1], F32)
        nmagcol = sbg("nmagcol", [P, 1], F32)
        zerocol = sbg("zerocol", [P, 1], F32)
        nhalfcol = sbg("nhalfcol", [P, 1], F32)
        nonecol = sbg("nonecol", [P, 1], F32)
        b1c = sbg("b1c_t", [P, 32], F32)
        b2c = sbg("b2c_t", [P, 8], F32)
        sw1c = sbg("sw1c", [P, 32], F32)
        sw2c = sbg("sw2c", [P, 8], F32)
        rsw1c = sbg("rsw1c", [P, 32], F32)
        rsw2c = sbg("rsw2c", [P, 8], F32)
        ss1c = sbg("ss1c", [P, 32], F32)
        ss2c = sbg("ss2c", [P, 8], F32)
        bq1c = sbg("bq1c", [P, 32], F32)
        bq2c = sbg("bq2c", [P, 8], F32)
        s1col = sbg("s1col", [P, 1], F32)
        rs1col = sbg("rs1col", [P, 1], F32)
        s2col = sbg("s2col", [P, 1], F32)
        rs2col = sbg("rs2col", [P, 1], F32)
        vmaxc = sbg("vmaxc", [P, 1], F32)
        hmaxc = sbg("hmaxc", [P, 1], F32)
        rmaxc = sbg("rmaxc", [P, 1], F32)
        meanb = sbg("meanb", [P, 4], F32)
        ex2b = sbg("ex2b", [P, 4], F32)
        msqb = sbg("msqb", [P, 4], F32)
        varb = sbg("varb", [P, 4], F32)
        sdb = sbg("sdb", [P, 4], F32)
        bn6 = sbg("bn6", [P, 48], F32)
        mvb = sbg("mvb", [P, 8], F32)
        vrow = sbg("vrow", [1, P], F32)
        scrow = sbg("scrow", [1, 8], F32)
        grow = sbg("grow", [1, 8], F32)
        row2 = sbg("row2", [1, 2], F32)
        rowst = sbg("rowst", [8, P], F32)

        C0 = dmac["C0"]
        pl.em("sync", lambda e: e.dma_start(out=idbf[:], in_=idbf_in), inc=C0, inck=16)
        pl.em("sync", lambda e: e.dma_start(out=idf[:], in_=idf_in), inc=C0, inck=16)
        pl.em("sync", lambda e: e.dma_start(out=trit[:], in_=tri_in), inc=C0, inck=16)
        pl.em("sync", lambda e: e.dma_start(out=b1c[:], in_=b1_in), inc=C0, inck=16)
        pl.em("sync", lambda e: e.dma_start(out=b2c[:], in_=b2_in), inc=C0, inck=16)
        for _k in ("idbf", "idf", "tri", "b1", "b2"):
            ev[_k] = (C0, C0.n)

        def _init(e):
            e.memset(onesrow[:], 1.0)
            e.memset(onecol[:], 1.0)
            e.memset(epscol[:], LNEPS)
            e.memset(nmagcol[:], -MAGIC)
            e.memset(zerocol[:], 0.0)
            e.memset(nhalfcol[:], -0.5)
            e.memset(nonecol[:], -1.0)
            e.memset(vmaxc[:], -3.0e38)
            return e.memset(hmaxc[:], -3.0e38)
        ev["init"] = (VE, pl.em("vector", _init, inc=VE))

        with ExitStack() as actx:
            sba = lambda n, s, d: actx.enter_context(nc.sbuf_tensor(n, s, d))
            wqb = sba("wqb", [P, 8 * D], BF16)
            wkb = sba("wkb", [P, 8 * D], BF16)
            wvb = sba("wvb", [P, 8 * D], BF16)
            vres = sba("vres", [P, NT * D], BF16)
            stw = sba("stw", [P, NT * 256], BF16)
            qfm = sba("qfm", [P, 8 * 256], BF16)
            kstr = [sba(f"kstr{i}", [P, D], BF16) for i in range(4)]
            fld = [sba(f"fld{i}", [P, 2 * D], F32) for i in range(2)]
            xin = [sba(f"xin{i}", [P, D], F32) for i in range(4)]
            ubf = sba("ubf", [P, D], BF16)
            ufm = sba("ufm", [P, D], BF16)
            qpb = sba("qpb", [P, D], BF16)
            kpb = sba("kpb", [P, D], BF16)
            kfmt = sba("kfmt", [P, D], BF16)
            rotb = sba("rotb", [P, 512], BF16)
            asb = [sba(f"asb{i}", [P, D], F32) for i in range(2)]
            x1t = asb
            s1c_ = sba("s1c_", [P, 4], F32)
            q1c_ = sba("q1c_", [P, 4], F32)
            s2c_ = sba("s2c_", [P, 2], F32)
            q2c_ = sba("q2c_", [P, 2], F32)
            s3c_ = sba("s3c_", [P, 2], F32)
            q3c_ = sba("q3c_", [P, 2], F32)
            rstd1 = sba("rstd1", [P, 4], F32)
            nmr1 = sba("nmr1", [P, 4], F32)
            rstd2 = sba("rstd2", [P, 2], F32)
            nmr2 = sba("nmr2", [P, 2], F32)
            rstd3 = sba("rstd3", [P, 2], F32)
            nmr3 = sba("nmr3", [P, 2], F32)
            wstg = [sba(f"wstg{i}", [P, 512], F32) for i in range(2)]
            wstg_free_ctr = [None, None]
            rswt = [sba(f"rswt{i}", [P, 512], F32) for i in range(2)]
            wibf = sba("wibf", [P, 512], BF16)
            rswrow = sba("rswrow", [1, 512], F32)

            for name, win, wsb in (("wq", wq_in, wqb), ("wk", wk_in, wkb), ("wv", wv_in, wvb)):
                _gv = pl.em(
                    "gpsimd",
                    lambda e, win=win, wsb=wsb: e.dma_start(
                        out=wsb[:].rearrange("p (c j) -> p c j", c=8),
                        in_=win.rearrange("(c p) j -> p c j", p=128)),
                    inc=GD, inck=16)
            for name in ("wq", "wk", "wv"):
                ev[f"{name}b"] = (GD, _gv)

            def ln_chain(eng_s, s_ap, q_ap, k, rstd_ap, nmr_ap, dsize, waits):
                pl.vdrain()
                pl.em("vector", lambda e: e.tensor_scalar(meanb[:, :k], s_ap, 1.0 / dsize, None, OP.mult),
                      waits=list(waits) + [ev.get("chain_busy", (SC, 0))])
                pl.em("vector", lambda e: e.tensor_scalar(ex2b[:, :k], q_ap, 1.0 / dsize, None, OP.mult))
                pl.vdrain()
                pl.em("vector", lambda e: e.tensor_mul(msqb[:, :k], meanb[:, :k], meanb[:, :k]))
                pl.vdrain()
                v1 = pl.em("vector", lambda e: e.tensor_tensor(varb[:, :k], ex2b[:, :k], msqb[:, :k], OP.subtract), inc=VE)
                pl.em("scalar", lambda e: e.activation(sdb[:, :k], varb[:, :k], AF.Ln, bias=epscol[:], scale=onecol[:]),
                      waits=[(VE, v1)])
                pl.sdrain()
                ev["chain_busy"] = (SC, SC.n)
                s1 = pl.em("scalar", lambda e: e.activation(rstd_ap, sdb[:, :k], AF.Exp, bias=zerocol[:], scale=nhalfcol[:]),
                           inc=SC)
                v2 = pl.em("vector", lambda e: e.scalar_tensor_tensor(nmr_ap, meanb[:, :k], -1.0, rstd_ap, OP.mult, OP.mult),
                           waits=[(SC, s1)], inc=VE)
                return v2

            def bn_chain(tile_aps, k, rstd_ap, nmr_ap, waits_list):
                for j in range(k):
                    w = list(waits_list[j]) + ([ev.get("chain_busy", (SC, 0))] if j == 0 else [])
                    pl.em("vector", lambda e, j=j: e.bn_stats(bn6[:, j * 12: j * 12 + 6],
                                                              tile_aps[j][:, 0:512]), waits=w)
                    pl.em("vector", lambda e, j=j: e.bn_stats(bn6[:, j * 12 + 6: j * 12 + 12],
                                                              tile_aps[j][:, 512:1024]))
                pl.vdrain()
                for j in range(k):
                    v1 = pl.em("vector", lambda e, j=j: e.bn_aggr(mvb[:, j * 2: j * 2 + 2],
                                                                  bn6[:, j * 12: j * 12 + 12]),
                               inc=VE if j == k - 1 else None)
                pl.em("scalar", lambda e: e.activation(sdb[:, :k], mvb[:, 1:2 * k:2], AF.Ln,
                                                       bias=epscol[:], scale=onecol[:]),
                      waits=[(VE, v1)])
                pl.sdrain()
                s1 = pl.em("scalar", lambda e: e.activation(rstd_ap, sdb[:, :k], AF.Exp,
                                                            bias=zerocol[:], scale=nhalfcol[:]), inc=SC)
                ev["chain_busy"] = (SC, s1)
                v2 = pl.em("vector", lambda e: e.scalar_tensor_tensor(nmr_ap, mvb[:, 0:2 * k:2], -1.0,
                                                                      rstd_ap, OP.mult, OP.mult),
                           waits=[(SC, s1)], inc=VE)
                return v2

            tr_free = [ev["init"][1], ev["init"][1]]
            tr_buf = [0]

            def trans8(src_bf, dst3d_fn, src_ready, extra_waits=()):
                b = tr_buf[0] % 2
                tr_buf[0] += 1
                waits = [src_ready, (VE, tr_free[b]), ev["idbf"]] + list(extra_waits)
                for dc in range(8):
                    t_inc = TE if dc == 7 else None
                    tv = pl.em("tensor",
                               lambda e, dc=dc, b=b, s=src_bf: e.transpose(
                                   PB[b][:, dc * 128:(dc + 1) * 128], s[:, dc * 128:(dc + 1) * 128], idbf[:]),
                               waits=waits if dc == 0 else (), inc=t_inc)
                fv = pl.tfence(TE)
                vv = pl.em("vector", lambda e, b=b: dst3d_fn(e, PB[b]), waits=[(TE, fv)], inc=VE)
                tr_free[b] = vv
                return vv

            wsteps = []
            wstg_free = [ev["init"][1], ev["init"][1]]

            def _mk_sw1(fc, hh):
                def f():
                    b = (2 * fc + hh) % 2
                    WGb = dmac[f"WG{b}"]
                    gv = pl.em("gpsimd",
                               lambda e: e.dma_start(out=wstg[b][:],
                                                     in_=w1n_in[fc * P:(fc + 1) * P, hh * 512:(hh + 1) * 512]),
                               waits=[(wstg_free_ctr[b] or VE, wstg_free[b])], inc=WGb, inck=16)
                    if hh == 0:
                        vv = pl.em("vector", lambda e: e.reduce_max(sw1c[:, fc:fc + 1], wstg[b][:], axis=AX,
                                                                    apply_absolute_value=True),
                                   waits=[(WGb, gv)], inc=VE)
                    else:
                        pl.em("vector", lambda e: e.reduce_max(rmaxc[:], wstg[b][:], axis=AX,
                                                               apply_absolute_value=True), waits=[(WGb, gv)])
                        pl.vdrain()
                        vv = pl.em("vector", lambda e: e.tensor_max(sw1c[:, fc:fc + 1], sw1c[:, fc:fc + 1], rmaxc[:]),
                                   inc=VE)
                    wstg_free[b] = vv
                    wstg_free_ctr[b] = VE
                return f

            for fc in range(32):
                for hh in range(2):
                    wsteps.append(_mk_sw1(fc, hh))

            def _mk_sw2(ec, q):
                def f():
                    b = (8 * ec + q) % 2
                    WGb = dmac[f"WG{b}"]
                    gv = pl.em("gpsimd",
                               lambda e: e.dma_start(out=wstg[b][:],
                                                     in_=w2n_in[ec * P:(ec + 1) * P, q * 512:(q + 1) * 512]),
                               waits=[(wstg_free_ctr[b] or VE, wstg_free[b])], inc=WGb, inck=16)
                    if q == 0:
                        vv = pl.em("vector", lambda e: e.reduce_max(sw2c[:, ec:ec + 1], wstg[b][:], axis=AX,
                                                                    apply_absolute_value=True),
                                   waits=[(WGb, gv)], inc=VE)
                    else:
                        pl.em("vector", lambda e: e.reduce_max(rmaxc[:], wstg[b][:], axis=AX,
                                                               apply_absolute_value=True), waits=[(WGb, gv)])
                        pl.vdrain()
                        vv = pl.em("vector", lambda e: e.tensor_max(sw2c[:, ec:ec + 1], sw2c[:, ec:ec + 1], rmaxc[:]),
                                   inc=VE)
                    wstg_free[b] = vv
                    wstg_free_ctr[b] = VE
                return f

            for ec in range(8):
                for q in range(8):
                    wsteps.append(_mk_sw2(ec, q))

            def _scales():
                pl.vdrain()
                pl.em("vector", lambda e: e.tensor_scalar(sw1c[:], sw1c[:], QEPS, 1.0 / 127.0, OP.max, OP.mult))
                pl.vdrain()
                vv = pl.em("vector", lambda e: e.tensor_scalar(sw2c[:], sw2c[:], QEPS, 1.0 / 127.0, OP.max, OP.mult),
                           inc=VE)
                pl.em("scalar", lambda e: e.activation(rsw1c[:], sw1c[:], AF.Ln, bias=zerocol[:], scale=onecol[:]),
                      waits=[(VE, vv)])
                pl.em("scalar", lambda e: e.activation(rsw2c[:], sw2c[:], AF.Ln, bias=zerocol[:], scale=onecol[:]))
                pl.sdrain()
                pl.em("scalar", lambda e: e.activation(rsw1c[:], rsw1c[:], AF.Exp, bias=zerocol[:], scale=nonecol[:]))
                v = pl.em("scalar", lambda e: e.activation(rsw2c[:], rsw2c[:], AF.Exp, bias=zerocol[:], scale=nonecol[:]),
                          inc=SC)
                ev["scales"] = (SC, v)
                if _DEBUG:
                    pl.em("sync", lambda e: e.dma_start(out=dbgSW1, in_=sw1c[:]), waits=[(VE, v)], inc=SY, inck=16)
                    pl.em("sync", lambda e: e.dma_start(out=dbgRSW1, in_=rsw1c[:]), inc=SY, inck=16)
                    pl.em("sync", lambda e: e.dma_start(out=dbgSW2, in_=sw2c[:]), inc=SY, inck=16)
            wsteps.append(_scales)

            rt_state = {"cur": 0, "free": [(VE, 0), (VE, 0)]}

            def _mk_rowtile(col_ap_fn):
                def f():
                    b = rt_state["cur"] ^ 1
                    rt_state["cur"] = b
                    tv = pl.em("tensor", lambda e: e.transpose(PF[4][:4, :P], col_ap_fn(), idf[:]),
                               waits=[ev["scales"], ev["idf"],
                                      ev.get("rswrow_free", (TE, 0)), rt_state["free"][b],
                                      (psq_free_ctr[0], psq_free[0]), (psq_free_ctr[1], psq_free[1])], inc=TE)
                    tv = pl.tfence(TE)
                    vv = pl.em("vector", lambda e: e.tensor_copy(rowst[:4, :], PF[4][:4, :P]),
                               waits=[(TE, tv), ev.get("rowst_free", (VE, 0))], inc=VE)
                    for cc in range(4):
                        g1 = pl.em("sync",
                                   lambda e, cc=cc: e.dma_start(out=rswrow[:, cc * P:(cc + 1) * P],
                                                                in_=rowst[cc:cc + 1, :]),
                                   waits=[(VE, vv)] if cc == 0 else [], inc=dmac["RR"], inck=16)
                    ev["rowst_free"] = (dmac["RR"], g1)
                    t1 = pl.em("tensor",
                               lambda e: e.matmul(PF[5][:, :512], onesrow[:], rswrow[:], start=True, stop=True),
                               waits=[(dmac["RR"], g1), (VE, ev["init"][1])], inc=TE)
                    t1 = pl.tfence(TE)
                    v1 = pl.em("vector", lambda e, b=b: e.tensor_copy(rswt[b][:], PF[5][:]), waits=[(TE, t1)], inc=VE)
                    psq_free[0] = v1
                    psq_free[1] = v1
                    psq_free_ctr[0] = VE
                    psq_free_ctr[1] = VE
                    ev["rswrow_free"] = (TE, t1)
                    ev["rowtile"] = (VE, v1)
                    if _DEBUG and ev.get("rt_dumped") is None:
                        ev["rt_dumped"] = True
                        pl.em("sync", lambda e, b=b: e.dma_start(out=dbgRT, in_=rswt[b][:]), waits=[(VE, v1)], inc=SY, inck=16)
                return f

            def _mk_wq(src_in, sp_out_fn, evkey):
                def f():
                    b = next(_wqb)
                    WGb = dmac[f"WG{b}"]
                    gv = pl.em("gpsimd", lambda e: e.dma_start(out=wstg[b][:], in_=src_in),
                               waits=[(wstg_free_ctr[b] or VE, wstg_free[b])], inc=WGb, inck=16)
                    b_rt = rt_state["cur"]
                    pl.em("vector", lambda e, b=b, b_rt=b_rt: e.tensor_mul(wstg[b][:], wstg[b][:], rswt[b_rt][:]),
                          waits=[(WGb, gv), ev["rowtile"]])
                    pl.vdrain()
                    vv = pl.em("vector", lambda e, b=b: e.tensor_scalar(wstg[b][:], wstg[b][:], 1.0, MAGIC,
                                                                        OP.mult, OP.add), inc=VE)
                    rt_state["free"][b_rt] = (VE, vv)
                    sv = pl.em("scalar", lambda e, b=b: e.activation(wibf[:], wstg[b][:], AF.Identity,
                                                                     bias=nmagcol[:], scale=onecol[:]),
                               waits=[(VE, vv), ev.get("wibf_free", (VE, 0))], inc=SC)
                    wstg_free[b] = sv
                    wstg_free_ctr[b] = SC
                    gv2 = pl.em("scalar", lambda e: e.dma_start(out=sp_out_fn(), in_=wibf[:]),
                                waits=[(SC, sv)], inc=dmac["WS"], inck=16)
                    ev[evkey] = (dmac["WS"], gv2)
                    ev["wibf_free"] = (dmac["WS"], gv2)
                return f

            _wqb_ctr = [0]
            def _next_wqb():
                while True:
                    _wqb_ctr[0] ^= 1
                    yield _wqb_ctr[0]
            _wqb = _next_wqb()

            for r in range(8):
                wsteps.append(_mk_rowtile(lambda r=r: rsw1c[:, r * 4:(r + 1) * 4]))
                for dc in range(8):
                    wsteps.append(_mk_wq(w1t_in[dc * P:(dc + 1) * P, r * 512:(r + 1) * 512],
                                         lambda dc=dc, r=r: w1i_sp[dc, :, r * 512:(r + 1) * 512],
                                         ("w1isp", r, dc)))
            for h in range(2):
                wsteps.append(_mk_rowtile(lambda h=h: rsw2c[:, h * 4:(h + 1) * 4]))
                for fc in range(32):
                    wsteps.append(_mk_wq(w2t_in[fc * P:(fc + 1) * P, h * 512:(h + 1) * 512],
                                         lambda fc=fc, h=h: w2i_sp[fc, :, h * 512:(h + 1) * 512],
                                         ("w2isp", h, fc)))

            wstep_iter = iter(wsteps)
            n_wsteps = len(wsteps)
            per_chunk = (n_wsteps + NCH - 1) // NCH

            psq_free = [ev["init"][1], ev["init"][1]]
            psq_free_ctr = [VE, VE]
            pss_free = [(VE, 0), (VE, 0), (VE, 0)]
            def emit_ln1_stats(c):
                cp = c % 2
                ws = []
                for j, t in enumerate((2 * c, 2 * c + 1)):
                    lw = [ev.get(("xin_free", t - 4), (VE, 0))]
                    XIb = dmac[f"XI{t % 4}"]
                    dv = pl.em("sync", lambda e, t=t, b=t % 4: e.dma_start(out=xin[b][:], in_=x_in[t * P:(t + 1) * P, :]),
                               waits=lw, inc=XIb, inck=16)
                    ev[("xin", t)] = (XIb, dv)
                    ws.append([(XIb, dv)])
                ev[("ln1cv", c)] = (VE, bn_chain(
                    [xin[(2 * c) % 4], xin[(2 * c + 1) % 4]], 2,
                    rstd1[:, cp * 2:cp * 2 + 2], nmr1[:, cp * 2:cp * 2 + 2], ws))

            def emit_tile_prod(c):
                pl.phase = "attn"
                cp = c % 2
                tiles = (2 * c, 2 * c + 1)
                cv = ev[("ln1cv", c)][1]
                if _DEBUG and c in (0, 1):
                    dst = dbgS0 if c == 0 else dbgS1
                    pl.em("sync", lambda e, dst=dst: e.dma_start(out=dst[:, 0:2], in_=s1c_[:]), waits=[(VE, cv)], inc=SY, inck=16)
                    pl.em("sync", lambda e, dst=dst: e.dma_start(out=dst[:, 2:4], in_=q1c_[:]), inc=SY, inck=16)
                    pl.em("sync", lambda e, dst=dst: e.dma_start(out=dst[:, 4:6], in_=rstd1[:]), inc=SY, inck=16)
                    pl.em("sync", lambda e, dst=dst: e.dma_start(out=dst[:, 6:8], in_=nmr1[:]), inc=SY, inck=16)
                for j, t in enumerate(tiles):
                    b = t % 4
                    uv = pl.em("scalar", lambda e, b=b, jj=cp * 2 + j: e.activation(ubf[:], xin[b][:], AF.Identity,
                                                                          bias=nmr1[:, jj:jj + 1], scale=rstd1[:, jj:jj + 1]),
                               waits=[(VE, cv), ev.get(("ubf_free", t - 1), (TE, 0))], inc=SC)
                    uf = trans8(ubf, lambda e, pb: e.tensor_copy(ufm[:], pb[:]), (SC, uv),
                                extra_waits=[ev.get(("ufm_free", t - 1), (TE, 0))])
                    ev[("ubf_free", t)] = (TE, TE.n)
                    if _DEBUG and t == 0:
                        pl.em("sync", lambda e: e.dma_start(out=dbgU, in_=ubf[:]), waits=[(SC, uv)], inc=SY, inck=16)
                        pl.em("sync", lambda e: e.dma_start(out=dbgUFM, in_=ufm[:]), waits=[(VE, uf)], inc=SY, inck=16)
                    for pi, (cf, sf) in enumerate(((fcq_in, fsq_in), (fck_in, fsk_in))):
                        fb = (2 * t + pi) % 2
                        fw = [ev.get(("fld_free", 2 * t + pi - 2), (VE, 0))]
                        FLb = dmac[f"FL{fb}"]
                        pl.em("sync", lambda e, fb=fb, cf=cf, t=t: e.dma_start(out=fld[fb][:, :D], in_=cf[t * P:(t + 1) * P, :]),
                              waits=fw, inc=FLb, inck=16)
                        dv = pl.em("sync", lambda e, fb=fb, sf=sf, t=t: e.dma_start(out=fld[fb][:, D:], in_=sf[t * P:(t + 1) * P, :]),
                                   inc=FLb, inck=16)
                        ev[("fld", 2 * t + pi)] = (FLb, dv)
                    for wi, (wsb, kind) in enumerate(((wqb, "q"), (wkb, "k"), (wvb, "v"))):
                        for h in range(2):
                            g = (wi * 2 + h)
                            pb_ = PF[4 + (g % 2)]
                            gfree = psq_free[g % 2]
                            gctr = psq_free_ctr[g % 2]
                            for dc in range(8):
                                waits = []
                                if dc == 0:
                                    waits = [(VE, uf), (gctr, gfree), (GD, ev[f"w{kind}b"][1])]
                                tv = pl.em("tensor",
                                           lambda e, pb_=pb_, dc=dc, wsb=wsb, h=h: e.matmul(
                                               pb_[:], ufm[:, dc * 128:(dc + 1) * 128],
                                               wsb[:, dc * D + h * 512: dc * D + (h + 1) * 512],
                                               start=(dc == 0), stop=(dc == 7), skip_group_check=True),
                                           waits=waits, inc=TE if dc == 7 else None)
                            fv = pl.tfence(TE)
                            if kind == "v":
                                sv = pl.em("scalar",
                                           lambda e, pb_=pb_, t=t, h=h: e.copy(
                                               vres[:, t * D + h * 512: t * D + (h + 1) * 512], pb_[:]),
                                           waits=[(TE, fv)], inc=SC)
                                psq_free[g % 2] = sv
                                psq_free_ctr[g % 2] = SC
                                if h == 1:
                                    ev[("vres", t)] = (SC, sv)
                            else:
                                dst = qpb if kind == "q" else kpb
                                fb = (2 * t + (0 if kind == "q" else 1)) % 2
                                fe = ev[("fld", 2 * t + (0 if kind == "q" else 1))]
                                pl.em("vector",
                                      lambda e, pb_=pb_, fb=fb, h=h, dst=dst: e.tensor_mul(
                                          dst[:, h * 512:(h + 1) * 512], pb_[:], fld[fb][:, h * 512:(h + 1) * 512]),
                                      waits=[(TE, fv), fe,
                                             ev.get(("dst_free", kind, t - 1), (TE, 0))])
                                pl.em("vector",
                                      lambda e, pb_=pb_, fb=fb, h=h: e.tensor_mul(
                                          rotb[:, 0:512:2], pb_[:, 1:512:2], fld[fb][:, D + h * 512: D + (h + 1) * 512][:, 0:512:2]))
                                pl.em("vector",
                                      lambda e, pb_=pb_, fb=fb, h=h: e.tensor_mul(
                                          rotb[:, 1:512:2], pb_[:, 0:512:2], fld[fb][:, D + h * 512: D + (h + 1) * 512][:, 1:512:2]))
                                pl.vdrain()
                                vv = pl.em("vector",
                                           lambda e, dst=dst, h=h: e.tensor_add(
                                               dst[:, h * 512:(h + 1) * 512], dst[:, h * 512:(h + 1) * 512], rotb[:]), inc=VE)
                                psq_free[g % 2] = vv
                                psq_free_ctr[g % 2] = VE
                                if h == 1:
                                    ev[("xpos", kind, t)] = (VE, vv)
                                    ev[("fld_free", 2 * t + (0 if kind == "q" else 1))] = (VE, vv)
                                    if _DEBUG and t == 0:
                                        pl.em("sync", lambda e, dst=dst, kind=kind: e.dma_start(
                                            out=dbgQP if kind == "q" else dbgKP, in_=dst[:]),
                                            waits=[(VE, vv)], inc=SY, inck=16)
                    qv = trans8(qpb,
                                lambda e, pb, j=j: e.tensor_copy(
                                    qfm[:].rearrange("p (c w t) -> p c w t", c=8, w=2)[:, :, j],
                                    pb[:].rearrange("p (c t) -> p c t", c=8)),
                                ev[("xpos", "q", t)],
                                extra_waits=[ev.get(("qfm_free", c - 1), (TE, 0))])
                    ev[("dst_free", "q", t)] = (TE, TE.n)
                    ev[("qfm", t)] = (VE, qv)
                    kv_ = trans8(kpb,
                                 lambda e, pb: e.tensor_copy(kfmt[:], pb[:]),
                                 ev[("xpos", "k", t)],
                                 extra_waits=[ev.get(("kfmt_free", t - 1), (SY, 0))])
                    ev[("dst_free", "k", t)] = (TE, TE.n)
                    dv = pl.em("sync", lambda e, t=t: e.dma_start(out=k_sp[t], in_=kfmt[:]),
                               waits=[(VE, kv_)], inc=dmac["KSP"], inck=16)
                    ev[("ksp", t)] = (dmac["KSP"], dv)
                    ev[("kfmt_free", t)] = (dmac["KSP"], dv)

            emit_ln1_stats(0)
            emit_tile_prod(0)
            for c in range(NCH):
                cp = c % 2
                tiles = (2 * c, 2 * c + 1)
                if c + 1 < NCH:
                    emit_ln1_stats(c + 1)
                pl.phase = "scores"
                nmax = 2 * c + 1
                pss_free[2] = (psq_free_ctr[0], psq_free[0])

                def emit_kload(cc, m):
                    if ("kload", cc, m) in ev:
                        return
                    kn = cc * (cc + 1) + m
                    mb = m % 4
                    KLb = dmac[f"KL{mb}"]
                    dv = pl.em("scalar", lambda e, mb=mb, m=m: e.dma_start(out=kstr[mb][:], in_=k_sp[m]),
                               waits=[ev[("ksp", m)], ev.get(("kstr_free", kn - 4), (TE, 0))],
                               inc=KLb, inck=16)
                    ev[("kload", cc, m)] = (KLb, dv)

                for m0 in range(min(4, nmax + 1)):
                    emit_kload(c, m0)

                def emit_scmm(m):
                    ms = m % 3
                    kn = c * (c + 1) + m
                    first_inc = None
                    for dc in range(8):
                        waits = []
                        if dc == 0:
                            waits = [ev[("kload", c, m)], pss_free[ms], (VE, ev[("qfm", tiles[1])][1])]
                        tv = pl.em("tensor",
                                   lambda e, ms=ms, mb=m % 4, dc=dc: e.matmul(
                                       PF[2 + ms][:, 0:256], kstr[mb][:, dc * 128:(dc + 1) * 128],
                                       qfm[:, dc * 256:(dc + 1) * 256],
                                       start=(dc == 0), stop=(dc == 7), skip_group_check=True),
                                   waits=waits, inc=TE if dc in (0, 7) else None)
                        if dc == 0:
                            first_inc = tv
                    ev[("scmm", c, m)] = (TE, tv)
                    ev[("kstr_free", kn)] = (TE, tv)
                    if m + 4 <= nmax:
                        emit_kload(c, m + 4)
                    return first_inc

                def emit_stev(m, lag_ev):
                    ms = m % 3
                    g = _gp(128 * (2 * c - m))
                    if m == 2 * c:
                        pl.em("vector",
                              lambda e, ms=ms, m=m: e.scalar_tensor_tensor(
                                  stw[:, m * 256: m * 256 + 128], PF[2 + ms][:, 0:128], 1.0, trit[:],
                                  OP.mult, OP.mult),
                              waits=[lag_ev, ev["tri"], ev.get(("stw_free", c - 1), (TE, 0))])
                        vv = pl.em("vector",
                                   lambda e, ms=ms, m=m: e.tensor_copy(
                                       stw[:, m * 256 + 128: (m + 1) * 256], PF[2 + ms][:, 128:256]), inc=VE)
                    elif m == 2 * c + 1:
                        vv = pl.em("vector",
                                   lambda e, ms=ms, m=m, g=_gp(-128): e.scalar_tensor_tensor(
                                       stw[:, m * 256 + 128: (m + 1) * 256], PF[2 + ms][:, 128:256], g, trit[:],
                                       OP.mult, OP.mult),
                                   waits=[lag_ev, ev["tri"], ev.get(("stw_free", c - 1), (TE, 0))], inc=VE)
                    else:
                        vv = pl.em("vector",
                                   lambda e, ms=ms, m=m, g=g: e.tensor_scalar(
                                       stw[:, m * 256: (m + 1) * 256], PF[2 + ms][:, 0:256], g, None, OP.mult),
                                   waits=[lag_ev, ev.get(("stw_free", c - 1), (TE, 0))], inc=VE)
                    ev[("stev", c, m)] = (VE, vv)
                    pss_free[ms] = (VE, vv)
                    if ms == 2:
                        psq_free[0] = vv
                        psq_free_ctr[0] = VE
                    if _DEBUG and c == 0:
                        pl.em("sync", lambda e, m=m: e.dma_start(out=dbgST[:, m * 256:(m + 1) * 256],
                                                                 in_=stw[:, m * 256:(m + 1) * 256]),
                              waits=[(VE, vv)], inc=SY, inck=16)

                def emit_av1(m):
                    av_first = None
                    for nsub in range(2):
                        nt_ = 2 * c + nsub
                        if nt_ < m:
                            continue
                        waits = [ev[("stev", c, m)], ev[("vres", m)]]
                        if m == 0:
                            waits.append(ev.get(("af", c - 1), (SC, 0)))
                        tv2 = pl.em("tensor",
                                    lambda e, nsub=nsub, m=m, nt_=nt_: e.matmul(
                                        PF[nsub][:], stw[:, m * 256 + nsub * 128: m * 256 + (nsub + 1) * 128],
                                        vres[:, m * D: m * D + 512],
                                        start=(m == 0), stop=(m == nt_), skip_group_check=True),
                                    waits=waits, inc=TE)
                        if av_first is None:
                            av_first = tv2
                    return av_first

                for m in range(nmax + 1):
                    fi = emit_scmm(m)
                    if m >= 1:
                        emit_stev(m - 1, (TE, fi))
                    if m >= 2:
                        emit_av1(m - 2)
                fi = emit_av1(nmax - 1)
                emit_stev(nmax, (TE, fi))
                emit_av1(nmax)
                ev[("avp1", c)] = (TE, pl.tfence(TE))
                a1 = pl.em("scalar", lambda e: e.copy(asb[0][:, 0:512], PF[0][:]),
                           waits=[(TE, ev[("avp1", c)][1]), ev.get(("asb_free", c - 1, 0), (VE, 0)),
                                  ev.get(("asb_free_sp", c - 1, 0), (VE, 0))], inc=SC)
                a2 = pl.em("scalar", lambda e: e.copy(asb[1][:, 0:512], PF[1][:]),
                           waits=[ev.get(("asb_free", c - 1, 1), (VE, 0)),
                                  ev.get(("asb_free_sp", c - 1, 1), (VE, 0))], inc=SC)
                ev[("aev1", c)] = (SC, a2)
                for m in range(nmax + 1):
                    for nsub in range(2):
                        nt_ = 2 * c + nsub
                        if nt_ < m:
                            continue
                        waits = []
                        if m == 0:
                            waits = [ev[("aev1", c)]]
                        last_av2 = (m == nmax and nsub == 1)
                        tv2 = pl.em("tensor",
                                    lambda e, nsub=nsub, m=m, nt_=nt_: e.matmul(
                                        PF[nsub][:], stw[:, m * 256 + nsub * 128: m * 256 + (nsub + 1) * 128],
                                        vres[:, m * D + 512: (m + 1) * D],
                                        start=(m == 0), stop=(m == nt_), skip_group_check=True),
                                    waits=waits, inc=TE if last_av2 else None)
                ev[("avp2", c)] = (TE, pl.tfence(TE))
                ev[("stw_free", c)] = (TE, TE.n)
                a1 = pl.em("scalar", lambda e: e.copy(asb[0][:, 512:], PF[0][:]),
                           waits=[(TE, ev[("avp2", c)][1])], inc=SC)
                a2 = pl.em("scalar", lambda e: e.copy(asb[1][:, 512:], PF[1][:]), inc=SC)
                ev[("af", c)] = (SC, a2)
                if _DEBUG and c == 0:
                    pl.em("sync", lambda e: e.dma_start(out=dbgA, in_=asb[0][:]), waits=[(SC, a1)], inc=SY, inck=16)

                if c + 1 < NCH:
                    emit_tile_prod(c + 1)
                    for m0 in range(4):
                        emit_kload(c + 1, m0)
                pl.phase = "epi"
                cv2 = bn_chain([asb[0], asb[1]], 2, rstd2[:], nmr2[:],
                               [[(SC, a1)], [ev[("af", c)]]])
                if _DEBUG and c == 0:
                    pl.em("sync", lambda e: e.dma_start(out=dbgS2[:, 0:2], in_=s2c_[:]), waits=[(VE, cv2)], inc=SY, inck=16)
                    pl.em("sync", lambda e: e.dma_start(out=dbgS2[:, 2:4], in_=q2c_[:]), inc=SY, inck=16)
                    pl.em("sync", lambda e: e.dma_start(out=dbgS2[:, 4:6], in_=rstd2[:]), inc=SY, inck=16)
                    pl.em("sync", lambda e: e.dma_start(out=dbgS2[:, 6:8], in_=nmr2[:]), inc=SY, inck=16)
                    pl.em("sync", lambda e: e.dma_start(out=dbgA1, in_=asb[1][:]), inc=SY, inck=16)
                x1vs = []
                for j in range(2):
                    t = tiles[j]
                    sv = pl.em("scalar", lambda e, j=j: e.activation(asb[j][:], asb[j][:], AF.Identity,
                                                                     bias=nmr2[:, j:j + 1], scale=rstd2[:, j:j + 1]),
                               waits=[(VE, cv2)], inc=SC)
                    x1v = pl.em("vector", lambda e, j=j, b=t % 4: e.tensor_add(x1t[j][:], asb[j][:], xin[b][:]),
                                waits=[(SC, sv)], inc=VE)
                    ev[("xin_free", t)] = (VE, x1v)
                    XSPj = dmac[f"XSP{j}"]
                    dv = pl.em("sync", lambda e, j=j, t=t: e.dma_start(out=x1_sp[t], in_=x1t[j][:]),
                               waits=[(VE, x1v)], inc=XSPj, inck=16)
                    ev[("x1sp", t)] = (XSPj, dv)
                    ev[("x1t_spill_prev", c, j)] = (XSPj, dv)
                    x1vs.append(x1v)
                cv3 = bn_chain([x1t[0], x1t[1]], 2, rstd3[:], nmr3[:],
                               [[(VE, x1vs[0])], [(VE, x1vs[1])]])
                for j in range(2):
                    t = tiles[j]
                    sv = pl.em("scalar", lambda e, j=j: e.activation(x1t[j][:], x1t[j][:], AF.Identity,
                                                                     bias=nmr3[:, j:j + 1], scale=rstd3[:, j:j + 1]),
                               waits=[(VE, cv3), ev[("x1sp", t)]], inc=SC)
                    VSPj = dmac[f"VSP{j}"]
                    dv = pl.em("sync", lambda e, j=j, t=t: e.dma_start(out=v_sp[t], in_=x1t[j][:]),
                               waits=[(SC, sv)], inc=VSPj, inck=16)
                    ev[("vsp", t)] = (VSPj, dv)
                    ev[("x1t_spill_prev", c, j)] = (VSPj, dv)
                    pl.em("vector", lambda e, j=j: e.reduce_max(rmaxc[:], x1t[j][:], axis=AX), waits=[(SC, sv)])
                    pl.vdrain()
                    vv = pl.em("vector", lambda e: e.tensor_max(vmaxc[:], vmaxc[:], rmaxc[:]), inc=VE)
                    ev[("asb_free", c, j)] = (VE, vv)
                    ev[("asb_free_sp", c, j)] = ev[("vsp", t)]
                ev[("vmax", c)] = (VE, vv)

                pl.phase = "wprep"
                for _ in range(per_chunk):
                    f = next(wstep_iter, None)
                    if f is not None:
                        f()

            for f in wstep_iter:
                f()

            attn_gate_vals = [(cc_, cc_.n) for cc_ in pl.all_ctrs if cc_ is not GE]

        pl.phase = "ar1"
        gv = pl.em("gpsimd", lambda e: e.dma_start(out=vmx_hbm, in_=vmaxc[:]),
                   waits=[ev[("vmax", NCH - 1)]], inc=GD, inck=16)
        gv = pl.em("gpsimd", lambda e: e.dma_start(out=vrow[:], in_=vmx_hbm.rearrange("p o -> o p")),
                   waits=[(GD, gv)], inc=GD, inck=16)
        vv = pl.em("vector", lambda e: e.memset(scrow[:], -3.0e38), waits=[(GD, gv)])
        pl.vdrain()
        vv = pl.em("vector", lambda e: e.reduce_max(scrow[:, 0:1], vrow[:], axis=AX), inc=VE)
        gv = pl.em("gpsimd", lambda e: e.dma_start(out=cc1i, in_=scrow[:]), waits=[(VE, vv)], inc=GD, inck=16)
        cv = pl.em("gpsimd", lambda e: e.collective_compute(
            "AllReduce", OP.max, replica_groups=[list(range(N_CORES))], ins=[cc1i], outs=[cc1o]),
            waits=[(GD, gv)], inc=GE)
        gv = pl.em("gpsimd", lambda e: e.dma_start(out=grow[:], in_=cc1o), waits=[(GE, cv)], inc=GD, inck=16)
        if _DEBUG:
            pl.em("gpsimd", lambda e: e.dma_start(out=dbg1, in_=grow[:]), waits=[(GD, gv)], inc=GD, inck=16)
        vv = pl.em("vector", lambda e: e.tensor_scalar(row2[:, 0:1], grow[:, 0:1], QEPS, 1.0 / 255.0, OP.max, OP.mult),
                   inc=VE, waits=[(GD, gv)])
        pl.em("scalar", lambda e: e.activation(grow[:, 1:2], row2[:, 0:1], AF.Ln, bias=zerocol[0:1, :], scale=onecol[0:1, :]),
              waits=[(VE, vv)])
        pl.sdrain()
        sv = pl.em("scalar", lambda e: e.activation(row2[:, 1:2], grow[:, 1:2], AF.Exp, bias=zerocol[0:1, :], scale=nonecol[0:1, :]),
                   inc=SC)
        tv = pl.em("tensor", lambda e: e.matmul(PF[4][:, 0:2], onesrow[:], row2[:], start=True, stop=True),
                   waits=[(VE, vv), (SC, sv)], inc=TE)
        tv = pl.tfence(TE)
        vv = pl.em("vector", lambda e: e.tensor_copy(s1col[:], PF[4][:, 0:1]), waits=[(TE, tv)])
        vv = pl.em("vector", lambda e: e.tensor_copy(rs1col[:], PF[4][:, 1:2]), inc=VE)
        ev["s1"] = (VE, vv)
        vv = pl.em("vector", lambda e: e.tensor_scalar(ss1c[:], sw1c[:], s1col[:], None, OP.mult), waits=[ev["s1"]])
        vv = pl.em("vector", lambda e: e.tensor_scalar(bq1c[:], rsw1c[:], rs1col[:], None, OP.mult), waits=[ev["scales"]])
        pl.vdrain()
        vv = pl.em("vector", lambda e: e.tensor_mul(bq1c[:], b1c[:], bq1c[:]), waits=[ev["b1"]])
        pl.vdrain()
        vv = pl.em("vector", lambda e: e.tensor_scalar(bq1c[:], bq1c[:], 1.0, MAGIC, OP.mult, OP.add), inc=VE)
        sv = pl.em("scalar", lambda e: e.activation(bq1c[:], bq1c[:], AF.Identity, bias=nmagcol[:], scale=onecol[:]),
                   waits=[(VE, vv)], inc=SC)
        vv = pl.em("vector", lambda e: e.tensor_mul(bq1c[:], bq1c[:], ss1c[:]), waits=[(SC, sv)], inc=VE)
        ev["bq1"] = (VE, vv)
        if _DEBUG:
            pl.em("sync", lambda e: e.dma_start(out=dbgFF[:, 0:32], in_=ss1c[:]), waits=[ev["bq1"]], inc=SY, inck=16)
            pl.em("sync", lambda e: e.dma_start(out=dbgFF[:, 32:64], in_=bq1c[:]), inc=SY, inck=16)
            pl.em("sync", lambda e: e.dma_start(out=dbgFF[:, 64:65], in_=s1col[:], allow_slow_non_contiguous=True), inc=SY, inck=16)
            pl.em("sync", lambda e: e.dma_start(out=dbgFF[:, 65:66], in_=rs1col[:], allow_slow_non_contiguous=True), inc=SY, inck=16)
            pl.em("sync", lambda e: e.dma_start(out=dbgFF[:, 66:67], in_=sw1c[:, 0:1], allow_slow_non_contiguous=True), inc=SY, inck=16)
            pl.em("sync", lambda e: e.dma_start(out=dbgFF[:, 67:68], in_=rsw1c[:, 0:1], allow_slow_non_contiguous=True), inc=SY, inck=16)

        attn_gate = attn_gate_vals
        with ExitStack() as fctx:
            sbf = lambda n, s, d: fctx.enter_context(nc.sbuf_tensor(n, s, d))
            w2i = sbf("w2i", [P, 32 * D], BF16)
            srow_t = sbf("srow_t", [P, D], F32)
            brow_t = sbf("brow_t", [P, D], F32)
            row1024a = sbf("row1024a", [1, D], F32)
            row1024b = sbf("row1024b", [1, D], F32)

            w2i_waits = list(attn_gate) + [ev[("w2isp", 1, 31)]]
            for dc8 in range(8):
                gv = pl.em("gpsimd",
                           lambda e, dc8=dc8: e.dma_start(
                               out=w2i[:, dc8 * 4096:(dc8 + 1) * 4096].rearrange("p (f e) -> p f e", e=D),
                               in_=w2i_sp[dc8 * 4:(dc8 + 1) * 4].rearrange("f p e -> p f e")),
                           waits=w2i_waits if dc8 == 0 else [], inc=dmac["W2L"], inck=16)
            ev["w2i"] = (dmac["W2L"], gv)

            with ExitStack() as f1ctx:
                sb1 = lambda n, s, d: f1ctx.enter_context(nc.sbuf_tensor(n, s, d))
                w1i = sb1("w1i", [P, 8 * FF], BF16)
                vload = [sb1(f"vload{i}", [P, D], F32) for i in range(2)]
                vtmp = sb1("vtmp", [P, D], F32)
                vqtm = sb1("vqtm", [P, D], BF16)
                vqfm = sb1("vqfm", [P, 8 * 512], BF16)
                hev = [sb1(f"hev{i}", [P, 512], F32) for i in range(2)]

                w1i_waits = list(attn_gate) + [ev[("w1isp", 7, 7)]]
                for dc in range(8):
                    gv = pl.em("gpsimd",
                               lambda e, dc=dc: e.dma_start(out=w1i[:, dc * FF:(dc + 1) * FF], in_=w1i_sp[dc]),
                               waits=w1i_waits if dc == 0 else [ev[("w1isp", 7, dc)]], inc=dmac["W1L"], inck=16)
                ev["w1i"] = (dmac["W1L"], gv)

                pl.phase = "ffn1"
                for tc in range(8):
                    for jj in range(4):
                        t = tc * 4 + jj
                        b = t % 2
                        lw = [ev[("vsp", t)], ev.get(("vload_free", t - 2), (VE, 0))] + (attn_gate if t == 0 else [])
                        VLb = dmac[f"VL{b}"]
                        dv = pl.em("scalar", lambda e, b=b, t=t: e.dma_start(out=vload[b][:], in_=v_sp[t]),
                                   waits=lw, inc=VLb, inck=16)
                        vv = pl.em("vector", lambda e, b=b: e.tensor_scalar(vtmp[:], vload[b][:], rs1col[:], MAGIC,
                                                                            OP.mult, OP.add),
                                   waits=[(VLb, dv), ev["s1"], ev.get(("vtmp_free", t - 1), (SC, 0))], inc=VE)
                        ev[("vload_free", t)] = (VE, vv)
                        sv = pl.em("scalar", lambda e: e.activation(vqtm[:], vtmp[:], AF.Relu,
                                                                    bias=nmagcol[:], scale=onecol[:]),
                                   waits=[(VE, vv), ev.get(("vqtm_free", t - 1), (TE, 0))], inc=SC)
                        ev[("vtmp_free", t)] = (SC, sv)
                        qv = trans8(vqtm,
                                    lambda e, pb, jj=jj: e.tensor_copy(
                                        vqfm[:].rearrange("p (c w t) -> p c w t", c=8, w=4)[:, :, jj],
                                        pb[:].rearrange("p (c t) -> p c t", c=8)),
                                    (SC, sv),
                                    extra_waits=[ev.get(("vqfm_free", tc - 1), (TE, 0))])
                        ev[("vqtm_free", t)] = (TE, TE.n)
                        ev[("vqfm", t)] = (VE, qv)
                        if _DEBUG and t == 0:
                            pl.em("sync", lambda e: e.dma_start(out=dbgVQ, in_=vqfm[:, 0:512]),
                                  waits=[(VE, qv)], inc=SY, inck=16)
                    for fc in range(32):
                        hb = fc % 2
                        for dc in range(8):
                            waits = []
                            if dc == 0:
                                waits = [(VE, ev[("vqfm", tc * 4 + 3)][1]), ev["w1i"],
                                         ev.get(("hev_free", tc * 32 + fc - 2), (VE, 0)),
                                         ev.get(("hev_free2", tc * 32 + fc - 2), (VE, 0))]
                            tv = pl.em("tensor",
                                       lambda e, hb=hb, dc=dc, fc=fc: e.matmul(
                                           PF[hb][:], w1i[:, dc * FF + fc * 128: dc * FF + (fc + 1) * 128],
                                           vqfm[:, dc * 512:(dc + 1) * 512],
                                           start=(dc == 0), stop=(dc == 7), skip_group_check=True),
                                       waits=waits, inc=TE if dc == 7 else None)
                        ffv = pl.tfence(TE)
                        sv = pl.em("scalar", lambda e, hb=hb, fc=fc: e.activation(
                            hev[hb][:], PF[hb][:], AF.Relu, bias=bq1c[:, fc:fc + 1], scale=ss1c[:, fc:fc + 1]),
                            waits=[(TE, ffv), ev["bq1"]], inc=SC)
                        HSb = dmac[f"HS{hb}"]
                        dv = pl.em("scalar", lambda e, hb=hb, fc=fc, tc=tc: e.dma_start(
                            out=h_sp[fc, :, tc * 512:(tc + 1) * 512], in_=hev[hb][:]),
                            waits=[(SC, sv)], inc=HSb, inck=16)
                        ev[("hsp", fc, tc)] = (HSb, dv)
                        ev[("hev_free", tc * 32 + fc)] = (HSb, dv)
                        pl.em("vector", lambda e, hb=hb: e.reduce_max(rmaxc[:], hev[hb][:], axis=AX),
                              waits=[(SC, sv)])
                        pl.vdrain()
                        vv = pl.em("vector", lambda e: e.tensor_max(hmaxc[:], hmaxc[:], rmaxc[:]), inc=VE)
                        ev[("hev_free2", tc * 32 + fc)] = (VE, vv)
                        ev[("vqfm_free", tc)] = (TE, TE.n)
                ev["ffn1_te"] = (TE, TE.n)
                ev["ffn1_ve"] = (VE, VE.n)
                ev["ffn1_sc"] = (SC, SC.n)
                ev["ffn1_sd"] = (dmac["HS0"], dmac["HS0"].n)
                ev["ffn1_sd2"] = (dmac["HS1"], dmac["HS1"].n)
                ev["hmax"] = (VE, VE.n)

            pl.phase = "ar2"
            gv = pl.em("gpsimd", lambda e: e.dma_start(out=hmx_hbm, in_=hmaxc[:]),
                       waits=[ev["hmax"]], inc=GD, inck=16)
            gv = pl.em("gpsimd", lambda e: e.dma_start(out=vrow[:], in_=hmx_hbm.rearrange("p o -> o p")),
                       waits=[(GD, gv)], inc=GD, inck=16)
            vv = pl.em("vector", lambda e: e.memset(scrow[:], -3.0e38), waits=[(GD, gv)])
            pl.vdrain()
            vv = pl.em("vector", lambda e: e.reduce_max(scrow[:, 0:1], vrow[:], axis=AX), inc=VE)
            gv = pl.em("gpsimd", lambda e: e.dma_start(out=cc2i, in_=scrow[:]), waits=[(VE, vv)], inc=GD, inck=16)
            cv = pl.em("gpsimd", lambda e: e.collective_compute(
                "AllReduce", OP.max, replica_groups=[list(range(N_CORES))], ins=[cc2i], outs=[cc2o]),
                waits=[(GD, gv)], inc=GE)
            gv = pl.em("gpsimd", lambda e: e.dma_start(out=grow[:], in_=cc2o), waits=[(GE, cv)], inc=GD, inck=16)
            if _DEBUG:
                pl.em("gpsimd", lambda e: e.dma_start(out=dbg2, in_=grow[:]), waits=[(GD, gv)], inc=GD, inck=16)
            vv = pl.em("vector", lambda e: e.tensor_scalar(row2[:, 0:1], grow[:, 0:1], QEPS, 1.0 / 255.0, OP.max, OP.mult),
                       inc=VE, waits=[(GD, gv)])
            pl.em("scalar", lambda e: e.activation(grow[:, 1:2], row2[:, 0:1], AF.Ln, bias=zerocol[0:1, :], scale=onecol[0:1, :]),
                  waits=[(VE, vv)])
            pl.sdrain()
            sv = pl.em("scalar", lambda e: e.activation(row2[:, 1:2], grow[:, 1:2], AF.Exp, bias=zerocol[0:1, :], scale=nonecol[0:1, :]),
                       inc=SC)
            tv = pl.em("tensor", lambda e: e.matmul(PF[4][:, 0:2], onesrow[:], row2[:], start=True, stop=True),
                       waits=[(VE, vv), (SC, sv), (TE, ev["ffn1_te"][1])], inc=TE)
            tv = pl.tfence(TE)
            vv = pl.em("vector", lambda e: e.tensor_copy(s2col[:], PF[4][:, 0:1]), waits=[(TE, tv)])
            vv = pl.em("vector", lambda e: e.tensor_copy(rs2col[:], PF[4][:, 1:2]), inc=VE)
            ev["s2"] = (VE, vv)
            vv = pl.em("vector", lambda e: e.tensor_scalar(ss2c[:], sw2c[:], s2col[:], None, OP.mult), waits=[ev["s2"]])
            vv = pl.em("vector", lambda e: e.tensor_scalar(bq2c[:], rsw2c[:], rs2col[:], None, OP.mult), waits=[ev["scales"]])
            pl.vdrain()
            vv = pl.em("vector", lambda e: e.tensor_mul(bq2c[:], b2c[:], bq2c[:]), waits=[ev["b2"]])
            pl.vdrain()
            vv = pl.em("vector", lambda e: e.tensor_scalar(bq2c[:], bq2c[:], 1.0, MAGIC, OP.mult, OP.add), inc=VE)
            sv = pl.em("scalar", lambda e: e.activation(bq2c[:], bq2c[:], AF.Identity, bias=nmagcol[:], scale=onecol[:]),
                       waits=[(VE, vv), (SC, ev["ffn1_sc"][1])], inc=SC)
            vv = pl.em("vector", lambda e: e.tensor_mul(bq2c[:], bq2c[:], ss2c[:]), waits=[(SC, sv)], inc=VE)
            ev["bq2"] = (VE, vv)
            tv = pl.em("tensor", lambda e: e.transpose(PF[4][:8, :P], ss2c[:], idf[:]),
                       waits=[ev["bq2"]], inc=TE)
            tv = pl.tfence(TE)
            vv = pl.em("vector", lambda e: e.tensor_copy(rowst[:], PF[4][:8, :P]),
                       waits=[(TE, tv), ev.get("rowst_free", (VE, 0))], inc=VE)
            for cc in range(8):
                gv = pl.em("gpsimd",
                           lambda e, cc=cc: e.dma_start(out=row1024a[:, cc * P:(cc + 1) * P], in_=rowst[cc:cc + 1, :]),
                           waits=[(VE, vv)] if cc == 0 else [], inc=GD, inck=16)
            ev["rowst_free"] = (GD, gv)
            tv = pl.em("tensor", lambda e: e.transpose(PF[5][:8, :P], bq2c[:], idf[:]),
                       waits=[(VE, vv)], inc=TE)
            tv = pl.tfence(TE)
            vv = pl.em("vector", lambda e: e.tensor_copy(rowst[:], PF[5][:8, :P]),
                       waits=[(TE, tv), ev.get("rowst_free", (VE, 0))], inc=VE)
            for cc in range(8):
                gv2 = pl.em("gpsimd",
                            lambda e, cc=cc: e.dma_start(out=row1024b[:, cc * P:(cc + 1) * P], in_=rowst[cc:cc + 1, :]),
                            waits=[(VE, vv)] if cc == 0 else [], inc=GD, inck=16)
            ev["rowst_free"] = (GD, gv2)
            t1 = pl.em("tensor", lambda e: e.matmul(PF[0][:], onesrow[:], row1024a[:, :512], start=True, stop=True),
                       waits=[(GD, gv), (SC, ev["ffn1_sc"][1]), (VE, ev["ffn1_ve"][1])])
            t1 = pl.em("tensor", lambda e: e.matmul(PF[1][:], onesrow[:], row1024a[:, 512:], start=True, stop=True))
            t1 = pl.em("tensor", lambda e: e.matmul(PF[2][:], onesrow[:], row1024b[:, :512], start=True, stop=True),
                       waits=[(GD, gv2)])
            t1 = pl.em("tensor", lambda e: e.matmul(PF[3][:], onesrow[:], row1024b[:, 512:], start=True, stop=True),
                       inc=TE)
            t1 = pl.tfence(TE)
            v1 = pl.em("vector", lambda e: e.tensor_copy(srow_t[:, :512], PF[0][:]), waits=[(TE, t1)])
            v1 = pl.em("vector", lambda e: e.tensor_copy(srow_t[:, 512:], PF[1][:]))
            v1 = pl.em("vector", lambda e: e.tensor_copy(brow_t[:, :512], PF[2][:]))
            v1 = pl.em("vector", lambda e: e.tensor_copy(brow_t[:, 512:], PF[3][:]), inc=VE)
            ev["rows"] = (VE, v1)
            if _DEBUG:
                pl.em("sync", lambda e: e.dma_start(out=dbgSR, in_=srow_t[:]), waits=[ev["rows"]], inc=SY, inck=16)
                pl.em("sync", lambda e: e.dma_start(out=dbgBR, in_=brow_t[:]), inc=SY, inck=16)

            ffn1_gate = [(TE, ev["ffn1_te"][1]), (VE, ev["ffn1_ve"][1]), (SC, ev["ffn1_sc"][1]),
                         ev["ffn1_sd"], ev["ffn1_sd2"]]
            with ExitStack() as f2ctx:
                sb2 = lambda n, s, d: f2ctx.enter_context(nc.sbuf_tensor(n, s, d))
                hq = [sb2(f"hq{i}", [P, 32 * 512], BF16) for i in range(2)]
                hload = [sb2(f"hload{i}", [P, 512], F32) for i in range(4)]
                htmp = [sb2(f"htmp{i}", [P, 512], F32) for i in range(2)]
                out2 = [sb2(f"out2{i}", [P, D], F32) for i in range(4)]
                sqo = sb2("sqo", [P, D], F32)
                x1re = [sb2(f"x1re{i}", [P, D], F32) for i in range(4)]
                outt = [sb2(f"outt{i}", [P, D], F32) for i in range(2)]
                s4c_ = sb2("s4c_", [P, 4], F32)
                q4c_ = sb2("q4c_", [P, 4], F32)
                rstd4 = sb2("rstd4", [P, 4], F32)
                nmr4 = sb2("nmr4", [P, 4], F32)

                pl.phase = "ffn2"
                for tw in range(8):
                    wb_ = tw % 2
                    for fc in range(32):
                        b = fc % 4
                        lw = [ev[("hsp", fc, tw)], ev.get(("hload_free", tw * 32 + fc - 4), (VE, 0))]
                        if tw == 0 and fc == 0:
                            lw += ffn1_gate
                        HLb = dmac[f"HL{b}"]
                        dv = pl.em("sync", lambda e, b=b, fc=fc, tw=tw: e.dma_start(
                            out=hload[b][:], in_=h_sp[fc, :, tw * 512:(tw + 1) * 512]),
                            waits=lw, inc=HLb, inck=16)
                        vv = pl.em("vector", lambda e, b=b: e.tensor_scalar(htmp[b % 2][:], hload[b][:], rs2col[:], MAGIC,
                                                                            OP.mult, OP.add),
                                   waits=[(HLb, dv), ev["s2"], ev.get(("htmp_free", tw * 32 + fc - 2), (SC, 0))], inc=VE)
                        ev[("hload_free", tw * 32 + fc)] = (VE, vv)
                        sv = pl.em("scalar", lambda e, b=b, fc=fc, wb_=wb_: e.activation(
                            hq[wb_][:, fc * 512:(fc + 1) * 512], htmp[b % 2][:], AF.Identity,
                            bias=nmagcol[:], scale=onecol[:]),
                            waits=[(VE, vv), ev.get(("hq_free", tw - 2), (TE, 0))], inc=SC)
                        ev[("htmp_free", tw * 32 + fc)] = (SC, sv)
                        ev[("hqfc", tw, fc)] = (SC, sv)
                    ev[("hq", tw)] = (SC, SC.n)
                    if _DEBUG and tw == 0:
                        pl.em("sync", lambda e, wb_=wb_: e.dma_start(out=dbgHQ, in_=hq[wb_][:, 0:512]),
                              waits=[ev[("hq", 0)]], inc=SY, inck=16)
                    ln4ws = []
                    for tt in range(4):
                        t = tw * 4 + tt
                        tb = tt % 2
                        XRt = dmac[f"XR{tt}"]
                        rv = pl.em("sync", lambda e, t=t, tt=tt: e.dma_start(out=x1re[tt][:], in_=x1_sp[t]),
                                   waits=[ev[("x1sp", t)], ev.get(("x1re_free", t - 4), (VE, 0))] +
                                         (attn_gate + ffn1_gate if t == 0 else []),
                                   inc=XRt, inck=16)
                        ev[("x1re", t)] = (XRt, rv)
                        pA = PF[0 + 2 * tb]
                        pB = PF[1 + 2 * tb]
                        for fc in range(32):
                            waits = [ev[("hqfc", tw, fc)]]
                            if fc == 0:
                                waits += [ev["w2i"],
                                          ev.get(("psO_free", t - 2), (VE, 0)), ev["rows"]]
                            pl.em("tensor", lambda e, pA=pA, fc=fc, tt=tt, wb_=wb_: e.matmul(
                                pA[:], hq[wb_][:, fc * 512 + tt * 128: fc * 512 + (tt + 1) * 128],
                                w2i[:, fc * D: fc * D + 512],
                                start=(fc == 0), stop=(fc == 31), skip_group_check=True),
                                waits=waits)
                            tv = pl.em("tensor", lambda e, pB=pB, fc=fc, tt=tt, wb_=wb_: e.matmul(
                                pB[:], hq[wb_][:, fc * 512 + tt * 128: fc * 512 + (tt + 1) * 128],
                                w2i[:, fc * D + 512: (fc + 1) * D],
                                start=(fc == 0), stop=(fc == 31), skip_group_check=True),
                                inc=TE if fc == 31 else None)
                        ev[("o2mm", t)] = (TE, tv)
                        if tt == 3:
                            ev[("hq_free", tw)] = (TE, tv)
                        ofv = pl.tfence(TE)
                        ovw = [(TE, ofv), ev.get(("out2_free", tw - 1, tt), (SY, 0))]
                        pl.em("vector", lambda e, pA=pA, tt=tt: e.tensor_mul(out2[tt][:, :512], pA[:], srow_t[:, :512]),
                              waits=ovw)
                        pl.em("vector", lambda e, pB=pB, tt=tt: e.tensor_mul(out2[tt][:, 512:], pB[:], srow_t[:, 512:]))
                        pl.vdrain()
                        pl.em("vector", lambda e, tt=tt: e.tensor_add(out2[tt][:, :512], out2[tt][:, :512], brow_t[:, :512]))
                        ov = pl.em("vector", lambda e, tt=tt: e.tensor_add(out2[tt][:, 512:], out2[tt][:, 512:], brow_t[:, 512:]),
                                   inc=VE)
                        ev[("psO_free", t)] = (VE, ov)
                        ln4ws.append([(VE, ov)])
                    cv4 = bn_chain([out2[0], out2[1], out2[2], out2[3]], 4,
                                   rstd4[:], nmr4[:], ln4ws)
                    if _DEBUG and tw == 0:
                        pl.em("sync", lambda e: e.dma_start(out=dbgO2, in_=out2[0][:]), waits=[(VE, cv4)], inc=SY, inck=16)
                        pl.em("sync", lambda e: e.dma_start(out=dbgS4[:, 0:4], in_=s4c_[:]), inc=SY, inck=16)
                        pl.em("sync", lambda e: e.dma_start(out=dbgS4[:, 4:8], in_=q4c_[:]), inc=SY, inck=16)
                        pl.em("sync", lambda e: e.dma_start(out=dbgS4[:, 8:12], in_=rstd4[:]), inc=SY, inck=16)
                        pl.em("sync", lambda e: e.dma_start(out=dbgS4[:, 12:16], in_=nmr4[:]), inc=SY, inck=16)
                    for tt in range(4):
                        t = tw * 4 + tt
                        tb = tt % 2
                        sv = pl.em("scalar", lambda e, tt=tt: e.activation(out2[tt][:], out2[tt][:], AF.Identity,
                                                                           bias=nmr4[:, tt:tt + 1], scale=rstd4[:, tt:tt + 1]),
                                   waits=[(VE, cv4)], inc=SC)
                        ov = pl.em("vector", lambda e, tb=tb, tt=tt: e.tensor_add(outt[tb][:], out2[tt][:], x1re[tt][:]),
                                   waits=[(SC, sv), ev[("x1re", t)],
                                          ev.get(("outt_free", t - 2), (VE, 0))], inc=VE)
                        ev[("x1re_free", t)] = (VE, ov)
                        ev[("out2_free", tw, tt)] = (VE, ov)
                        OTb = dmac[f"OT{tb}"]
                        dv = pl.em("sync", lambda e, tb=tb, t=t: e.dma_start(out=out_hbm[t * P:(t + 1) * P, :], in_=outt[tb][:]),
                                   waits=[(VE, ov)], inc=OTb, inck=16)
                        ev[("outt_free", t)] = (OTb, dv)

                pl.wait_only("sync", [(dmac["OT0"], dmac["OT0"].n), (dmac["OT1"], dmac["OT1"].n)])

        with nc.Block() as block:
            @block.sync
            def _(eng):
                for s in pl.steps["sync"]:
                    s(eng)

            @block.scalar
            def _(eng):
                for s in pl.steps["scalar"]:
                    s(eng)

            @block.vector
            def _(eng):
                for s in pl.steps["vector"]:
                    s(eng)

            @block.tensor
            def _(eng):
                for s in pl.steps["tensor"]:
                    s(eng)

            @block.gpsimd
            def _(eng):
                for s in pl.steps["gpsimd"]:
                    s(eng)

    return nc


_CACHE = {}


def kernel(L0_attn_X, L0_attn_W_Q, L0_attn_W_K, L0_attn_W_V,
           L0_ffn_w1, L0_ffn_b1, L0_ffn_w2, L0_ffn_b2):
    key = ("nc", _DEBUG, repr(sorted(_SERIALIZE)) if isinstance(_SERIALIZE, set) else _SERIALIZE)
    if key not in _CACHE:
        _CACHE[key] = _build()
    if "const" not in _CACHE:
        _CACHE["const"] = _host_constants()
    nc = _CACHE[key]
    cst = _CACHE["const"]

    X = np.asarray(L0_attn_X, dtype=np.float32)
    shared = {
        "wq": np.ascontiguousarray(np.asarray(L0_attn_W_Q, np.float32)),
        "wk": np.ascontiguousarray(np.asarray(L0_attn_W_K, np.float32)),
        "wv": np.ascontiguousarray(np.asarray(L0_attn_W_V, np.float32)),
        "w1n": np.ascontiguousarray(np.asarray(L0_ffn_w1, np.float32)),
        "w1t": np.ascontiguousarray(np.asarray(L0_ffn_w1, np.float32).T),
        "w2n": np.ascontiguousarray(np.asarray(L0_ffn_w2, np.float32)),
        "w2t": np.ascontiguousarray(np.asarray(L0_ffn_w2, np.float32).T),
        "b1c": np.ascontiguousarray(np.asarray(L0_ffn_b1, np.float32).reshape(32, P).T),
        "b2c": np.ascontiguousarray(np.asarray(L0_ffn_b2, np.float32).reshape(8, P).T),
        **cst,
    }
    in_maps = [dict(shared, x=np.ascontiguousarray(X[b])) for b in range(N_CORES)]
    res = run_bass_kernel_spmd(nc, in_maps, core_ids=list(range(N_CORES)))
    if _DEBUG:
        _CACHE["dbg"] = res.results
    return np.stack([res.results[b]["out"] for b in range(N_CORES)], axis=0)


if __name__ == "__main__":
    nc = _build()
    print("build ok")
